# version 03
# baseline (speedup 1.0000x reference)
import contextlib
import numpy as np
import ml_dtypes

import concourse.bass_utils as _bu

if not getattr(_bu, "_dge_patched", False):
    _orig_walrus_args = _bu.get_walrus_args

    def _patched_walrus_args(arch, tmpdir, **kw):
        return _orig_walrus_args(arch, tmpdir, **kw) + [
            "--dge-levels=io,spill_reload,scalar_dynamic_offset,"
            "vector_dynamic_offsets,dynamic_size,dst_reduce",
            "--enable-ldw-opt=true"
        ]

    _bu.get_walrus_args = _patched_walrus_args
    _bu._dge_patched = True

import concourse.bass as bass
import concourse.mybir as mybir
from concourse.bass_utils import run_bass_kernel_spmd
from concourse.bass_core_rust import IndirectOffsetOnAxis

F32 = mybir.dt.float32
BF16 = mybir.dt.bfloat16
I32 = mybir.dt.int32
U32 = mybir.dt.uint32
AF = mybir.ActivationFunctionType
ALU = mybir.AluOpType

NCORES = 8
T = 2048
NT = 16
D = 1024
DC = 8
INTER = 2048
NM = 16
E = 16384
EPS_LN = 1e-5
EPS_ATT = 1e-8
NEG = -1.0e9

LVLS = [1, 2, 3, 4, 5]
LVL_NCOARSE = {l: T >> l for l in range(1, 9)}
LVL_PACK_OFF = {1: 0, 2: 512, 3: 768, 4: 896, 5: 1024}
LVL_PACK_N = {1: 512, 2: 256, 3: 128, 4: 64, 5: 32}
PACK_W = 1152
GROUP_LVL = [1, 1, 1, 1, 2, 2, 3, 4, 5]
CL = {l: 0.125 * (0.25 ** l) for l in range(0, 9)}
GP_OFF = {6: 0, 7: 32, 8: 48}
HEADS = 16
DH = 64

ENGINES = ("tensor", "vector", "scalar", "sync", "gpsimd")
DEBUG = []
_DBG_TENSORS = {}


class Ctx:

    def __init__(self, nc):
        self.nc = nc
        self.streams = {e: [] for e in ENGINES}
        self.cnt = {}
        self.wm = {e: {} for e in ENGINES}
        self.wr = {}
        self.rd = {}
        for e in ENGINES:
            self.cnt["e_" + e] = 0
        for q in ("sync", "gpsimd"):
            self.cnt["q_" + q] = 0
        self.cnt["cc"] = 0

    def _wait(self, eng, sem, val):
        if self.wm[eng].get(sem, 0) >= val:
            return
        self.wm[eng][sem] = val
        self.streams[eng].append(("wait", sem, val))

    def _dep(self, eng, rd_keys, wr_keys):
        need = {}
        for k in rd_keys:
            for s, v in self.wr.get(k, ()):
                need[s] = max(need.get(s, 0), v)
        for k in wr_keys:
            for s, v in self.wr.get(k, ()):
                need[s] = max(need.get(s, 0), v)
            for s, v in self.rd.get(k, {}).items():
                need[s] = max(need.get(s, 0), v)
        for s, v in need.items():
            self._wait(eng, s, v)

    def _rec(self, rd_keys, wr_keys, sem, val):
        for k in rd_keys:
            self.rd.setdefault(k, {})[sem] = val
        for k in wr_keys:
            self.wr[k] = [(sem, val)]
            self.rd[k] = {}

    def settle(self, keys, sems_):
        for k in keys:
            self.wr[k] = [(s, self.cnt[s]) for s in sems_ if self.cnt.get(s, 0) > 0]
            self.rd[k] = {}

    def op(self, eng, fn, rd=(), wr=()):
        self._dep(eng, rd, wr)
        sem = "e_" + eng
        self.cnt[sem] += 1
        self.streams[eng].append(("op", fn, sem, 1))
        self._rec(rd, wr, sem, self.cnt[sem])

    def dma(self, eng, fn, rd=(), wr=(), sem=None):
        self._dep(eng, rd, wr)
        sem = sem or ("q_" + eng)
        self.cnt[sem] = self.cnt.get(sem, 0) + 16
        self.streams[eng].append(("op", fn, sem, 16))
        self._rec(rd, wr, sem, self.cnt[sem])

    def cc(self, fn, rd=(), wr=()):
        self._dep("gpsimd", rd, wr)
        self.cnt["cc"] += 1
        self.streams["gpsimd"].append(("op", fn, "cc", 1))
        self._rec(rd, wr, "cc", self.cnt["cc"])

    def mms(self, items):
        sem = "e_tensor"
        for fn, rds, pk, first, last in items:
            self._dep("tensor", rds, (pk,) if first else ())
            self.cnt[sem] += 1
            self.streams["tensor"].append(("op", fn, sem, 1))
            self._rec(rds, (pk,) if last else (), sem, self.cnt[sem])

    def mm_group(self, parts, psum_key):
        sem = "e_tensor"
        n = len(parts)
        for i, (fn, rds) in enumerate(parts):
            self._dep("tensor", rds, (psum_key,) if i == 0 else ())
            self.cnt[sem] += 1
            self.streams["tensor"].append(("op", fn, sem, 1))
            self._rec(rds, (psum_key,) if i == n - 1 else (), sem, self.cnt[sem])

    def barrier(self, omit=()):
        for e in ENGINES:
            for s in sorted(self.cnt):
                if s in omit:
                    continue
                if self.cnt[s] > 0:
                    self._wait(e, s, self.cnt[s])
        self.wr = {}
        self.rd = {}

    def emit(self, sems, block):
        def mk(eng):
            def body(_obj):
                for item in self.streams[eng]:
                    if item[0] == "wait":
                        _, s, v = item
                        getattr(self.nc, eng).wait_ge(sems[s], v)
                    else:
                        _, fn, s, inc = item
                        fn().then_inc(sems[s], inc)
            return body
        for eng in ENGINES:
            if self.streams[eng]:
                getattr(block, eng)(mk(eng))


def _host_consts(core_id):
    b, c = divmod(core_id, 4)
    ident = np.eye(128, dtype=ml_dtypes.bfloat16)

    M0 = np.full((128, 128), NEG, np.float32)
    for i in range(128):
        blk = i // 16
        M0[i, blk * 16: blk * 16 + (i % 16) + 1] = 0.0
    MC = np.full((128, 128), NEG, np.float32)
    for i in range(128):
        blk = i // 16
        MC[i, blk * 16:(blk + 1) * 16] = 0.0

    iota64 = np.tile(np.arange(64, dtype=np.float32), (128, 1))
    ones_bf = np.ones((1, 128), dtype=ml_dtypes.bfloat16)

    ScC = np.zeros((PACK_W, T), np.float32)
    for l in LVLS:
        ncoarse = LVL_NCOARSE[l]
        nblk = ncoarse // 16
        oddpos = []
        for blk in range(nblk):
            if blk % 2 == 1:
                oddpos.extend(range(blk * 16, blk * 16 + 16))
        for j, cpos in enumerate(oddpos):
            fine = cpos * (1 << l) + (1 << l) - 1
            ScC[LVL_PACK_OFF[l] + j, fine] = 1.0
    ov = []
    for g in range(9):
        for t in range(NT):
            if ScC[g * 128:(g + 1) * 128, t * 128:(t + 1) * 128].any():
                ov.append((g, t))
    ScC_bf = ScC.astype(ml_dtypes.bfloat16)

    Sc678 = np.zeros((NT, 3, 16, 128), np.float32)
    for i in range(16):
        fine = (16 + i) * 64 + 63
        Sc678[fine // 128, 0, i, fine % 128] = 1.0
    if c % 2 == 1:
        for i in range(16):
            fine = i * 128 + 127
            Sc678[fine // 128, 1, i, fine % 128] = 1.0
    if c >= 2:
        for i in range(8):
            fine = i * 256 + 255
            Sc678[fine // 128, 2, i, fine % 128] = 1.0
    o678 = [t for t in range(NT) if Sc678[t].any()]
    Sc678_bf = Sc678.astype(ml_dtypes.bfloat16)

    Sel = np.zeros((128, 32), np.float32)
    if c % 2 == 1:
        src = (4 * b + (c - 1)) * 16
        for i in range(16):
            Sel[src + i, i] = 1.0
    if c >= 2:
        for j in range(16):
            chunk = j // 8
            i = j % 8
            src = (4 * b + chunk) * 16
            Sel[src + 2 * i, 16 + j] = 1.0
            Sel[src + 2 * i + 1, 16 + j] = 1.0
    Sel_bf = Sel.astype(ml_dtypes.bfloat16)

    return dict(ident=ident, M0=M0, MC=MC, iota64=iota64, ones_bf=ones_bf,
                ScC=ScC_bf, ov=ov, Sc678=Sc678_bf, o678=o678, Sel=Sel_bf)


def build_nc(meta, stages=3):
    nc = bass.Bass("TRN2", target_bir_lowering=False)
    ctx = Ctx(nc)

    x_in = nc.dram_tensor("x_in", [T, D], F32, kind="ExternalInput").ap()
    wqkvT = nc.dram_tensor("wqkvT", [D, 3 * D], F32, kind="ExternalInput").ap()
    woutT_i = nc.dram_tensor("woutT", [D, D], F32, kind="ExternalInput").ap()
    bout = nc.dram_tensor("bout", [1, D], F32, kind="ExternalInput").ap()
    wqT = nc.dram_tensor("wqT", [D, 256], F32, kind="ExternalInput").ap()
    keysT = nc.dram_tensor("keysT", [128, 2, 128], F32, kind="ExternalInput").ap()
    tbl_in = nc.dram_tensor("tbl_in", [E, 2 * D], F32, kind="ExternalInput").ap()
    wgT = nc.dram_tensor("wgT", [D, INTER], F32, kind="ExternalInput").ap()
    wuT = nc.dram_tensor("wuT", [D, INTER], F32, kind="ExternalInput").ap()
    wdT = nc.dram_tensor("wdT", [INTER, D], F32, kind="ExternalInput").ap()
    ident_in = nc.dram_tensor("ident", [128, 128], BF16, kind="ExternalInput").ap()
    ident32_in = nc.dram_tensor("ident32", [128, 128], F32, kind="ExternalInput").ap()
    m0_in = nc.dram_tensor("m0c", [128, 128], F32, kind="ExternalInput").ap()
    mc_in = nc.dram_tensor("mcc", [128, 128], F32, kind="ExternalInput").ap()
    iota_in = nc.dram_tensor("iota64", [128, 64], F32, kind="ExternalInput").ap()
    ones_in = nc.dram_tensor("ones_bf", [1, 128], BF16, kind="ExternalInput").ap()
    scc_in = nc.dram_tensor("scc", [PACK_W, T], BF16, kind="ExternalInput").ap()
    sc678_in = nc.dram_tensor("sc678", [NT, 3, 16, 128], BF16, kind="ExternalInput").ap()
    onescol_in = nc.dram_tensor("ones_col", [128, 1], BF16, kind="ExternalInput").ap()
    sel_in = nc.dram_tensor("sel", [128, 32], BF16, kind="ExternalInput").ap()

    out = nc.dram_tensor("out", [T, D], F32, kind="ExternalOutput").ap()

    dbg_outs = {}

    def tap(name, ap_fn, shape, dt, rd):
        if name not in DEBUG:
            return
        o = nc.dram_tensor("dbg_" + name, shape, dt, kind="ExternalOutput").ap()
        dbg_outs[name] = (shape, dt)
        ctx.dma("sync", lambda: nc.sync.dma_start(out=o, in_=ap_fn()), rd=rd, wr=())

    tbl_bf = nc.dram_tensor("tbl_bf", [E, 2 * D], BF16)
    wqkvT_bf = nc.dram_tensor("wqkvT_bf", [D, 3 * D], BF16)
    woutT_bf = nc.dram_tensor("woutT_bf", [D, D], BF16)
    wgT_bf = nc.dram_tensor("wgT_bf", [D, INTER], BF16)
    wuT_bf = nc.dram_tensor("wuT_bf", [D, INTER], BF16)
    wdT_bf = nc.dram_tensor("wdT_bf", [INTER, D], BF16)
    hbf_hbm = nc.dram_tensor("hbf_hbm", [T, D], BF16)
    x2_hbm = nc.dram_tensor("x2_hbm", [T, D], F32)
    x3_hbm = nc.dram_tensor("x3_hbm", [T, D], F32)
    qfm_hbm = nc.dram_tensor("qfm_hbm", [128, DC, T], BF16)
    kfm_hbm = nc.dram_tensor("kfm_hbm", [128, DC, T], BF16)
    vtm_hbm = nc.dram_tensor("vtm_hbm", [128, NT, D], BF16)
    agin_hbm = nc.dram_tensor("agin_hbm", [16, D], F32)
    agout_hbm = nc.dram_tensor("agout_hbm", [128, D], F32)

    estack = contextlib.ExitStack()

    with estack:
        def gsb(name, shape, dt):
            return estack.enter_context(nc.sbuf_tensor(name, shape, dt))

        ident = gsb("identc", [128, 128], BF16)
        ident32 = gsb("identc32", [128, 128], F32)
        m0c = gsb("m0cs", [128, 128], F32)
        mcc = gsb("mccs", [128, 128], F32)
        iota64 = gsb("iota64s", [128, 64], F32)
        onesb = gsb("onesbs", [1, 128], BF16)
        onescol = gsb("onescols", [128, 1], BF16)
        keysT_sb = gsb("keysTs", [128, 2, 128], F32)
        bout_sb = gsb("bouts", [1, D], BF16)
        selc = gsb("selcs", [128, 32], BF16)
        castb = [gsb(f"castb{i}", [128, D], BF16) for i in range(2)]

        ctx.dma("sync", lambda: nc.sync.dma_start(out=ident[:], in_=ident_in), wr=("ident",), sem="s_const")
        ctx.dma("sync", lambda: nc.sync.dma_start(out=ident32[:], in_=ident32_in), wr=("ident32",), sem="s_const")
        ctx.dma("sync", lambda: nc.sync.dma_start(out=m0c[:], in_=m0_in), wr=("m0c",), sem="s_const")
        ctx.dma("sync", lambda: nc.sync.dma_start(out=mcc[:], in_=mc_in), wr=("mcc",), sem="s_const")
        ctx.dma("sync", lambda: nc.sync.dma_start(out=iota64[:], in_=iota_in), wr=("iota64",), sem="s_const")
        ctx.dma("sync", lambda: nc.sync.dma_start(out=onesb[:], in_=ones_in), wr=("onesb",), sem="s_const")
        ctx.dma("sync", lambda: nc.sync.dma_start(out=onescol[:], in_=onescol_in), wr=("onescol",), sem="s_const")
        ctx.dma("sync", lambda: nc.sync.dma_start(out=selc[:], in_=sel_in), wr=("selc",), sem="s_const")
        ctx.dma("sync", lambda: nc.sync.dma_start(out=keysT_sb[:], in_=keysT), wr=("keysT",), sem="s_const")
        ctx.dma("gpsimd", lambda: nc.gpsimd.dma_start(out=bout_sb[:], in_=bout), wr=("bout",), sem="s_cb0")

        def cast_hbm(name, src_ap, dst_t, rows, cols):
            nch = rows // 128
            ncc = (cols + D - 1) // D
            src_r = src_ap.rearrange("(c p) d -> c p d", p=128, c=nch)
            dst_r = dst_t.ap().rearrange("(c p) d -> c p d", p=128, c=nch)
            ji = 0
            for i in range(nch):
                for j in range(ncc):
                    c0, c1 = j * D, min((j + 1) * D, cols)
                    bkey = f"castb{ji % 2}"
                    bslice = castb[ji % 2][:, 0:c1 - c0]
                    ctx.dma("gpsimd", lambda i=i, bs=bslice, c0=c0, c1=c1:
                        nc.gpsimd.dma_start(out=bs, in_=src_r[i][:, c0:c1]),
                        rd=(), wr=(bkey,), sem=f"s_cb{ji % 2}")
                    ctx.dma("gpsimd", lambda i=i, bs=bslice, c0=c0, c1=c1:
                        nc.gpsimd.dma_start(out=dst_r[i][:, c0:c1], in_=bs),
                        rd=(bkey,), wr=(name,), sem=f"s_cs{ji % 2}")
                    ji += 1
            ctx.settle((name,), ("s_cs0", "s_cs1"))

        cast_hbm("tbl", tbl_in, tbl_bf, E, 2 * D)
        cast_hbm("wg", wgT, wgT_bf, D, INTER)
        cast_hbm("wu", wuT, wuT_bf, D, INTER)
        cast_hbm("wd", wdT, wdT_bf, INTER, D)
        cast_hbm("wqkv", wqkvT, wqkvT_bf, D, 3 * D)
        cast_hbm("wout", woutT_i, woutT_bf, D, D)

        def ffn_phase(tag, x_src, x_dst):
            with contextlib.ExitStack() as ph:
                def psb(name, shape, dt):
                    return ph.enter_context(nc.sbuf_tensor(tag + name, shape, dt))

                x_sb = psb("x", [128, NT, D], F32)
                wqT_sb = psb("wqTs", [128, DC, 256], F32)
                ctx.dma("sync", lambda: nc.sync.dma_start(
                    out=wqT_sb[:], in_=wqT.rearrange("(c p) m -> p c m", p=128)),
                    wr=("wqT",), sem="s_const")
                ctx.dma("sync", lambda: nc.sync.dma_start(
                    out=x_sb[:], in_=x_src.rearrange("(t p) d -> p t d", p=128)),
                    wr=("x_sb",), sem="s_x")

                flat_i32 = psb("flat", [128, NT, 32], I32)
                wsm = psb("wsm", [128, NT * 32], F32)
                hidden = psb("hid", [128, NM, INTER], BF16)

                with contextlib.ExitStack() as ph2:
                    def psb2(name, shape, dt):
                        return ph2.enter_context(nc.sbuf_tensor(tag + name, shape, dt))

                    yT = psb2("yT", [128, DC, T], BF16)

                    with contextlib.ExitStack() as pa:
                        def pal(name, shape, dt):
                            return pa.enter_context(nc.sbuf_tensor(tag + name, shape, dt))
                        ybf = [pal(f"ybf{i}", [128, D], BF16) for i in range(2)]
                        lnsc = pal("lnsc", [128, D], F32)

                        st_m = pal("stm", [128, 1], F32)
                        st_v = pal("stv", [128, 1], F32)
                        st_r = pal("str", [128, 1], F32)
                        yt32 = pal("yt32", [128, DC, 128], F32)
                        q32t = pal("q32t", [128, 2, 128], F32)
                        sx8 = pal("sx8", [128, 8], F32)
                        sy8 = pal("sy8", [128, 8], F32)
                        ixu = pal("ixu", [128, 8], U32)
                        iyu = pal("iyu", [128, 8], U32)
                        ixf = pal("ixf", [128, 8], F32)
                        iyf = pal("iyf", [128, 8], F32)
                        cs64 = pal("cs64", [128, 64], F32)
                        ci64 = pal("ci64", [128, 64], F32)
                        vals8 = pal("vals8", [128, 8], F32)
                        posu = pal("posu", [128, 8], U32)
                        posf = pal("posf", [128, 8], F32)
                        jk64 = pal("jk64", [128, 64], F32)
                        e8 = pal("e8", [128, 8], F32)
                        ssum = pal("ssum", [128, 1], F32)
                        flat_f = pal("flatf", [128, 32], F32)
                        ps_s = [pa.enter_context(
                            nc.psum_tensor(tag + f"pss{i}", [128, 128], F32))
                            for i in range(2)]
                        ps_tr = pa.enter_context(
                            nc.psum_tensor(tag + "ptr", [128, 128], BF16))
                        ps_t32 = pa.enter_context(
                            nc.psum_tensor(tag + "ptr32", [128, 128], F32))
                        ps_q2 = pa.enter_context(
                            nc.psum_tensor(tag + "psq2", [128, 128], F32))
                        hbf_r = hbf_hbm.ap().rearrange("(t p) d -> p t d", p=128)

                        for t in range(NT):
                            yk = f"ybf{t % 2}"
                            xt = x_sb[:, t, :]
                            ctx.op("vector", lambda xt=xt: nc.vector.reduce_sum(
                                st_m[:], xt, axis=mybir.AxisListType.X),
                                rd=("x_sb",), wr=("st_m",))
                            ctx.op("vector", lambda: nc.vector.tensor_scalar_mul(
                                st_m[:], st_m[:], 1.0 / D), rd=("st_m",), wr=("st_m",))
                            ctx.op("vector", lambda xt=xt: nc.vector.tensor_scalar_sub(
                                lnsc[:], xt, st_m[:]), rd=("x_sb", "st_m"), wr=("lnsc",))
                            ctx.op("vector", lambda yb=ybf[t % 2]: nc.vector.scalar_tensor_tensor(
                                out=yb[:], in0=lnsc[:], scalar=1.0, in1=lnsc[:],
                                op0=ALU.mult, op1=ALU.mult, accum_out=st_v[:]),
                                rd=("lnsc",), wr=(yk, "st_v"))
                            ctx.op("vector", lambda: nc.vector.tensor_scalar(
                                out=st_v[:], in0=st_v[:], scalar1=1.0 / D, scalar2=EPS_LN,
                                op0=ALU.mult, op1=ALU.add), rd=("st_v",), wr=("st_v",))
                            ctx.op("vector", lambda: nc.vector.reciprocal(
                                st_v[:], st_v[:]), rd=("st_v",), wr=("st_v",))
                            ctx.op("scalar", lambda: nc.scalar.activation(
                                out=st_r[:], in_=st_v[:], func=AF.Sqrt),
                                rd=("st_v",), wr=("st_r",))
                            ctx.op("vector", lambda: nc.vector.tensor_scalar_mul(
                                lnsc[:], lnsc[:], st_r[:]), rd=("lnsc", "st_r"), wr=("lnsc",))
                            ctx.op("vector", lambda yb=ybf[t % 2]: nc.vector.tensor_copy(
                                yb[:], lnsc[:]), rd=("lnsc",), wr=(yk,))
                            ctx.dma("sync", lambda t=t, yb=ybf[t % 2]: nc.sync.dma_start(
                                out=hbf_r[:, t, :], in_=yb[:]), rd=(yk,), wr=(f"hbf{t}",))
                            for cch in range(DC):
                                ctx.op("tensor", lambda yb=ybf[t % 2], cch=cch:
                                    nc.tensor.transpose(
                                        ps_tr[:], yb[:, cch * 128:(cch + 1) * 128], ident[:]),
                                    rd=(yk, "ident"), wr=("ps_tr",))
                                ctx.op("vector", lambda cch=cch, t=t: nc.vector.tensor_copy(
                                    yT[:, cch, t * 128:(t + 1) * 128], ps_tr[:]),
                                    rd=("ps_tr",), wr=("yT",))
                                ctx.op("tensor", lambda cch=cch: nc.tensor.transpose(
                                    ps_t32[:], lnsc[:, cch * 128:(cch + 1) * 128], ident32[:]),
                                    rd=("lnsc", "ident32"), wr=("ps_t32",))
                                ctx.op("vector", lambda cch=cch: nc.vector.tensor_copy(
                                    yt32[:, cch, :], ps_t32[:]),
                                    rd=("ps_t32",), wr=("yt32",))
                            for ofc in range(2):
                                parts = [(lambda kd=kd, ofc=ofc: nc.tensor.matmul(
                                    ps_q2[:], wqT_sb[:, kd, ofc * 128:(ofc + 1) * 128],
                                    yt32[:, kd, :],
                                    start=(kd == 0), stop=(kd == DC - 1)),
                                    ("wqT", "yt32")) for kd in range(DC)]
                                ctx.mm_group(parts, "ps_q2")
                                ctx.op("vector", lambda ofc=ofc: nc.vector.tensor_copy(
                                    q32t[:, ofc, :], ps_q2[:]),
                                    rd=("ps_q2",), wr=("q32t",))
                            for h in range(4):
                                base = h * 32
                                tp = (96, 0) if base == 96 else None
                                pairs = ((0, sx8, ixu, ixf), (1, sy8, iyu, iyf))
                                for p, s8, iu, if_ in pairs:
                                    pk = f"ps_s{p}"
                                    ctx.mm_group([(lambda p=p, b_=base, tp=tp:
                                        nc.tensor.matmul(
                                            ps_s[p][:],
                                            q32t[b_:b_ + 32, p, :],
                                            keysT_sb[b_:b_ + 32, p, :],
                                            start=True, stop=True, tile_position=tp),
                                        ("q32t", "keysT"))], pk)
                                    ctx.op("vector", lambda p=p, s8=s8: nc.vector.max(
                                        out=s8[:], in_=ps_s[p][:]), rd=(pk,), wr=(f"s8_{p}",))
                                    ctx.op("vector", lambda p=p, s8=s8, iu=iu:
                                        nc.vector.max_index(
                                            out=iu[:], in_max=s8[:], in_values=ps_s[p][:]),
                                        rd=(pk, f"s8_{p}"), wr=(f"iu_{p}",))
                                    ctx.op("vector", lambda iu=iu, if_=if_:
                                        nc.vector.tensor_copy(if_[:], iu[:]),
                                        rd=(f"iu_{p}",), wr=(f"if_{p}",))
                                ctx.op("vector", lambda: nc.vector.tensor_scalar_mul(
                                    ixf[:], ixf[:], 128.0), rd=("if_0",), wr=("if_0",))
                                for i in range(8):
                                    ctx.op("vector", lambda i=i: nc.vector.tensor_scalar_add(
                                        cs64[:, i * 8:(i + 1) * 8], sy8[:], sx8[:, i:i + 1]),
                                        rd=("s8_0", "s8_1"), wr=("cs64",))
                                    ctx.op("vector", lambda i=i: nc.vector.tensor_scalar_add(
                                        ci64[:, i * 8:(i + 1) * 8], iyf[:], ixf[:, i:i + 1]),
                                        rd=("if_0", "if_1"), wr=("ci64",))
                                ctx.op("vector", lambda: nc.vector.max(
                                    out=vals8[:], in_=cs64[:]), rd=("cs64",), wr=("vals8",))
                                ctx.op("vector", lambda: nc.vector.max_index(
                                    out=posu[:], in_max=vals8[:], in_values=cs64[:]),
                                    rd=("cs64", "vals8"), wr=("posu",))
                                ctx.op("vector", lambda: nc.vector.tensor_copy(
                                    posf[:], posu[:]), rd=("posu",), wr=("posf",))
                                for sx in range(8):
                                    ctx.op("vector", lambda sx=sx, h=h:
                                        nc.vector.scalar_tensor_tensor(
                                            out=jk64[:], in0=iota64[:],
                                            scalar=posf[:, sx:sx + 1], in1=ci64[:],
                                            op0=ALU.is_equal, op1=ALU.mult,
                                            accum_out=flat_f[:, h * 8 + sx:h * 8 + sx + 1]),
                                        rd=("iota64", "posf", "ci64"),
                                        wr=("jk64", "flat_f"))
                                ctx.op("vector", lambda: nc.vector.tensor_scalar_sub(
                                    e8[:], vals8[:], vals8[:, 0:1]), rd=("vals8",), wr=("e8",))
                                ctx.op("scalar", lambda: nc.scalar.activation(
                                    out=e8[:], in_=e8[:], func=AF.Exp, accum_out=ssum[:]),
                                    rd=("e8",), wr=("e8", "ssum"))
                                ctx.op("vector", lambda: nc.vector.reciprocal(
                                    ssum[:], ssum[:]), rd=("ssum",), wr=("ssum",))
                                ctx.op("vector", lambda t=t, h=h: nc.vector.tensor_scalar(
                                    out=wsm[:, t * 32 + h * 8: t * 32 + (h + 1) * 8],
                                    in0=e8[:], scalar1=ssum[:], scalar2=0.5,
                                    op0=ALU.mult, op1=ALU.mult),
                                    rd=("e8", "ssum"), wr=("wsm",))
                            ctx.op("vector", lambda t=t: nc.vector.tensor_copy(
                                flat_i32[:, t, :], flat_f[:]), rd=("flat_f",), wr=("flat_i32",))

                    if tag == "f1":
                        tap("flat", lambda: flat_i32[:], [128, NT, 32], I32, ("flat_i32",))
                        tap("wsm", lambda: wsm[:], [128, NT * 32], F32, ("wsm",))
                    ctx.barrier(omit=("q_gpsimd",))
                    with contextlib.ExitStack() as pc:
                        def pcl(name, shape, dt):
                            return pc.enter_context(nc.sbuf_tensor(tag + name, shape, dt))
                        wg_sb = [pcl(f"wg{i}", [128, DC, 128], BF16) for i in range(2)]
                        wu_sb = [pcl(f"wu{i}", [128, DC, 128], BF16) for i in range(2)]
                        sg4 = [pcl(f"sg4{i}", [128, 512], BF16) for i in range(4)]
                        ps_g4 = [pc.enter_context(
                            nc.psum_tensor(tag + f"psg4{i}", [128, 512], F32)) for i in range(4)]
                        wg_r = wgT_bf.ap().rearrange("(c p) m -> p c m", p=128)
                        wu_r = wuT_bf.ap().rearrange("(c p) m -> p c m", p=128)

                        for m in range(NM):
                            wgk, wuk = f"wg{m % 2}", f"wu{m % 2}"
                            ctx.dma("sync", lambda m=m, b=wg_sb[m % 2]: nc.sync.dma_start(
                                out=b[:], in_=wg_r[:, :, m * 128:(m + 1) * 128]),
                                rd=("wg",), wr=(wgk,), sem=f"w_g{m % 2}")
                            ctx.dma("sync", lambda m=m, b=wu_sb[m % 2]: nc.sync.dma_start(
                                out=b[:], in_=wu_r[:, :, m * 128:(m + 1) * 128]),
                                rd=("wu",), wr=(wuk,), sem=f"w_u{m % 2}")
                            items = []
                            for kd in range(DC):
                                for tb in range(4):
                                    items.append((lambda kd=kd, m=m, tb=tb:
                                        nc.tensor.matmul(
                                            ps_g[tb % 4] if False else ps_g4[tb][:],
                                            wg_sb[m % 2][:, kd, :],
                                            yT[:, kd, tb * 512:(tb + 1) * 512],
                                            start=(kd == 0), stop=(kd == DC - 1)),
                                        (wgk, "yT"), f"ps_g4{tb}",
                                        kd == 0, kd == DC - 1))
                            ctx.mms(items)
                            for tb in range(4):
                                ctx.op("scalar", lambda tb=tb: nc.scalar.activation(
                                    out=sg4[tb][:], in_=ps_g4[tb][:], func=AF.Silu),
                                    rd=(f"ps_g4{tb}",), wr=(f"sg4{tb}",))
                            items = []
                            for kd in range(DC):
                                for tb in range(4):
                                    items.append((lambda kd=kd, m=m, tb=tb:
                                        nc.tensor.matmul(
                                            ps_g4[tb][:],
                                            wu_sb[m % 2][:, kd, :],
                                            yT[:, kd, tb * 512:(tb + 1) * 512],
                                            start=(kd == 0), stop=(kd == DC - 1)),
                                        (wuk, "yT", f"sg4{tb}"), f"ps_g4{tb}",
                                        kd == 0, kd == DC - 1))
                            ctx.mms(items)
                            for tb in range(4):
                                ctx.op("vector", lambda m=m, tb=tb: nc.vector.tensor_mul(
                                    hidden[:, m, tb * 512:(tb + 1) * 512],
                                    sg4[tb][:], ps_g4[tb][:]),
                                    rd=(f"sg4{tb}", f"ps_g4{tb}"), wr=("hidden",))

                ctx.barrier()
                with contextlib.ExitStack() as ph3:
                    wdT_sb = ph3.enter_context(
                        nc.sbuf_tensor(tag + "wdTs", [128, NM, D], BF16))
                    ps_d = [ph3.enter_context(
                        nc.psum_tensor(tag + f"psd{i}", [128, 512], F32)) for i in range(2)]
                    ctx.dma("sync", lambda: nc.sync.dma_start(
                        out=wdT_sb[:], in_=wdT_bf.ap().rearrange("(c p) m -> p c m", p=128)),
                        rd=("wd",), wr=("wdT_sb",), sem="s_wd")
                    for t in range(NT):
                        items = []
                        for m in range(NM):
                            for half in range(2):
                                items.append((lambda m=m, t=t, half=half:
                                    nc.tensor.matmul(
                                        ps_d[half][:], hidden[:, m, t * 128:(t + 1) * 128],
                                        wdT_sb[:, m, half * 512:(half + 1) * 512],
                                        start=(m == 0), stop=(m == NM - 1)),
                                    ("hidden", "wdT_sb"), f"ps_d{half}",
                                    m == 0, m == NM - 1))
                        ctx.mms(items)
                        for half in range(2):
                            ctx.op("vector", lambda t=t, half=half:
                                nc.vector.scalar_tensor_tensor(
                                    out=x_sb[:, t, half * 512:(half + 1) * 512],
                                    in0=ps_d[half][:], scalar=0.5,
                                    in1=x_sb[:, t, half * 512:(half + 1) * 512],
                                    op0=ALU.mult, op1=ALU.add),
                                rd=(f"ps_d{half}", "x_sb"), wr=("x_sb",))


                ctx.barrier()
                with contextlib.ExitStack() as pd:
                    def pdl(name, shape, dt):
                        return pd.enter_context(nc.sbuf_tensor(tag + name, shape, dt))
                    ew = pdl("ew", [128, NT * 32], F32)
                    pair = [pdl(f"pair{i}", [128, 2 * D], BF16) for i in range(8)]
                    hb = [pdl(f"hb{i}", [128, D], BF16) for i in range(2)]
                    dj = pdl("dj", [128, D], BF16)
                    silu8 = pdl("silu8", [128, 8], F32)
                    wfin = pdl("wfin", [128, 8], F32)
                    hbf_r = hbf_hbm.ap().rearrange("(t p) d -> p t d", p=128)

                    for t in range(NT):
                        hk = f"hb{t % 2}"
                        ctx.dma("sync", lambda t=t, b=hb[t % 2]: nc.sync.dma_start(
                            out=b[:], in_=hbf_r[:, t, :]), rd=(f"hbf{t}",), wr=(hk,),
                            sem=f"h_b{t % 2}")
                        for h in range(4):
                            for k in range(8):
                                sslot = h * 8 + k
                                pk = f"pair{sslot % 8}"
                                ctx.dma("gpsimd", lambda t=t, sslot=sslot,
                                    pb=pair[sslot % 8]: nc.gpsimd.indirect_dma_start(
                                        pb[:], None, tbl_bf.ap(),
                                        IndirectOffsetOnAxis(
                                            ap=flat_i32[:, t, sslot:sslot + 1], axis=0)),
                                    rd=("flat_i32", "tbl"), wr=(pk,),
                                    sem=f"g_p{sslot % 8}")
                                ctx.op("vector", lambda t=t, sslot=sslot,
                                    pb=pair[sslot % 8], b=hb[t % 2]:
                                    nc.vector.scalar_tensor_tensor(
                                        out=dj[:], in0=pb[:, 0:D], scalar=1.0, in1=b[:],
                                        op0=ALU.mult, op1=ALU.mult,
                                        accum_out=ew[:, t * 32 + sslot:t * 32 + sslot + 1]),
                                    rd=(pk, hk), wr=("dj", "ew"))
                            ctx.op("scalar", lambda t=t, h=h: nc.scalar.activation(
                                out=silu8[:], in_=ew[:, t * 32 + h * 8:t * 32 + h * 8 + 8],
                                func=AF.Silu), rd=("ew",), wr=("silu8",))
                            ctx.op("vector", lambda t=t, h=h: nc.vector.tensor_mul(
                                wfin[:], silu8[:],
                                wsm[:, t * 32 + h * 8:t * 32 + h * 8 + 8]),
                                rd=("silu8", "wsm"), wr=("wfin",))
                            for k in range(8):
                                sslot = h * 8 + k
                                pk = f"pair{sslot % 8}"
                                ctx.op("vector", lambda t=t, sslot=sslot, k=k,
                                    pb=pair[sslot % 8]: nc.vector.scalar_tensor_tensor(
                                        out=x_sb[:, t, :], in0=pb[:, D:2 * D],
                                        scalar=wfin[:, k:k + 1], in1=x_sb[:, t, :],
                                        op0=ALU.mult, op1=ALU.add),
                                    rd=(pk, "wfin", "x_sb"), wr=("x_sb",))

                if tag == "f1":
                    tap("hid", lambda: hidden[:], [128, NM, INTER], BF16, ("hidden",))
                    tap("xd", lambda: x_sb[:], [128, NT, D], F32, ("x_sb",))
                    tap("hbf", lambda: hbf_hbm.ap(), [T, D], BF16, ("hbf15",))
                ctx.dma("sync", lambda: nc.sync.dma_start(
                    out=x_dst.rearrange("(t p) d -> p t d", p=128), in_=x_sb[:]),
                    rd=("x_sb",), wr=("x_dst",))
            ctx.barrier()

        ffn_phase("f1", x_in, out if stages == 1 else x2_hbm.ap())

        if stages >= 2:
            attn_env = dict(
                nc=nc, ctx=ctx, meta=meta, ident=ident, ident32=ident32,
                m0c=m0c, mcc=mcc,
                onesb=onesb, onescol=onescol, bout_sb=bout_sb, selc=selc,
                x2_hbm=x2_hbm, x3_hbm=x3_hbm, out=out,
                hbf_hbm=hbf_hbm, qfm_hbm=qfm_hbm, kfm_hbm=kfm_hbm, vtm_hbm=vtm_hbm,
                agin_hbm=agin_hbm, agout_hbm=agout_hbm,
                wqkvT_bf=wqkvT_bf, woutT_bf=woutT_bf,
                scc_in=scc_in, sc678_in=sc678_in)
            build_attention(attn_env, x_dst_is_out=(stages == 2))

        if stages >= 3:
            ffn_phase("f2", x3_hbm.ap(), out)

        ctx.streams["gpsimd"].append(("wait", "q_sync", ctx.cnt["q_sync"]))
        ctx.wm["gpsimd"]["q_sync"] = ctx.cnt["q_sync"]

        sems = {}
        for sname in sorted(ctx.cnt):
            if ctx.cnt[sname] > 0:
                sems[sname] = estack.enter_context(nc.semaphore(sname))
        with nc.Block() as block:
            ctx.emit(sems, block)

    return nc



def _ov_pairs():
    import numpy as _np
    ScC = _np.zeros((PACK_W, T), _np.float32)
    for l in LVLS:
        nblk = LVL_NCOARSE[l] // 16
        oddpos = []
        for blk in range(nblk):
            if blk % 2 == 1:
                oddpos.extend(range(blk * 16, blk * 16 + 16))
        for j, cpos in enumerate(oddpos):
            ScC[LVL_PACK_OFF[l] + j, cpos * (1 << l) + (1 << l) - 1] = 1.0
    ov = {}
    for t in range(NT):
        ov[t] = [g for g in range(9)
                 if ScC[g * 128:(g + 1) * 128, t * 128:(t + 1) * 128].any()]
    return ov

OV = _ov_pairs()


def build_attention(env, x_dst_is_out):
    nc = env["nc"]; ctx = env["ctx"]
    ident = env["ident"]; ident32 = env["ident32"]
    m0c = env["m0c"]; mcc = env["mcc"]; onesb = env["onesb"]
    onescol = env["onescol"]
    bout_sb = env["bout_sb"]; selc = env["selc"]
    x2_hbm = env["x2_hbm"]; x3_hbm = env["x3_hbm"]; out = env["out"]
    qfm_hbm = env["qfm_hbm"]; kfm_hbm = env["kfm_hbm"]; vtm_hbm = env["vtm_hbm"]
    agin_hbm = env["agin_hbm"]; agout_hbm = env["agout_hbm"]
    wqkvT_bf = env["wqkvT_bf"]; woutT_bf = env["woutT_bf"]
    scc_in = env["scc_in"]; sc678_in = env["sc678_in"]
    tag = "at"

    x2_r = x2_hbm.ap().rearrange("(t p) d -> p t d", p=128)
    wq_r = wqkvT_bf.ap().rearrange("(c p) m -> p c m", p=128)
    wo_r = woutT_bf.ap().rearrange("(c p) m -> p c m", p=128)

    with contextlib.ExitStack() as pa:
        def al(name, shape, dt):
            return pa.enter_context(nc.sbuf_tensor(tag + name, shape, dt))

        qc = al("qc", [128, DC, PACK_W], BF16)
        kc = al("kc", [128, DC, PACK_W], BF16)
        vc = al("vc", [128, 9, 1025], BF16)
        qg = al("qg", [128, DC, 40], BF16)
        kg = al("kg", [128, DC, 48], BF16)
        vg = al("vg", [16, 3, 1024], BF16)
        normT = al("nT", [128, DC, T], BF16)

        with contextlib.ExitStack() as p1:
            def a1(name, shape, dt):
                return p1.enter_context(nc.sbuf_tensor(tag + name, shape, dt))
            yT = a1("yT", [128, DC, T], BF16)

            with contextlib.ExitStack() as p2:
                def a2(name, shape, dt):
                    return p2.enter_context(nc.sbuf_tensor(tag + name, shape, dt))
                xt2 = [a2(f"xt{i}", [128, D], F32) for i in range(2)]
                ybf = [a2(f"ybf{i}", [128, D], BF16) for i in range(2)]
                lnsc = a2("lnsc", [128, D], F32)
                st_m = a2("stm", [128, 1], F32)
                st_v = a2("stv", [128, 1], F32)
                st_r = a2("str", [128, 1], F32)
                ps_tr = p2.enter_context(nc.psum_tensor(tag + "ptr", [128, 128], BF16))

                for t in range(NT):
                    xk, yk = f"xt{t % 2}", f"ybf{t % 2}"
                    ctx.dma("sync", lambda t=t, b=xt2[t % 2]: nc.sync.dma_start(
                        out=b[:], in_=x2_r[:, t, :]), rd=("x_dst",), wr=(xk,),
                        sem=f"s_xt{t % 2}")
                    xt = xt2[t % 2]
                    ctx.op("vector", lambda xt=xt: nc.vector.reduce_sum(
                        st_m[:], xt[:], axis=mybir.AxisListType.X), rd=(xk,), wr=("st_m",))
                    ctx.op("vector", lambda: nc.vector.tensor_scalar_mul(
                        st_m[:], st_m[:], 1.0 / D), rd=("st_m",), wr=("st_m",))
                    ctx.op("vector", lambda xt=xt: nc.vector.tensor_scalar_sub(
                        lnsc[:], xt[:], st_m[:]), rd=(xk, "st_m"), wr=("lnsc",))
                    ctx.op("vector", lambda yb=ybf[t % 2]: nc.vector.scalar_tensor_tensor(
                        out=yb[:], in0=lnsc[:], scalar=1.0, in1=lnsc[:],
                        op0=ALU.mult, op1=ALU.mult, accum_out=st_v[:]),
                        rd=("lnsc",), wr=(yk, "st_v"))
                    ctx.op("vector", lambda: nc.vector.tensor_scalar(
                        out=st_v[:], in0=st_v[:], scalar1=1.0 / D, scalar2=EPS_LN,
                        op0=ALU.mult, op1=ALU.add), rd=("st_v",), wr=("st_v",))
                    ctx.op("vector", lambda: nc.vector.reciprocal(
                        st_v[:], st_v[:]), rd=("st_v",), wr=("st_v",))
                    ctx.op("scalar", lambda: nc.scalar.activation(
                        out=st_r[:], in_=st_v[:], func=AF.Sqrt), rd=("st_v",), wr=("st_r",))
                    ctx.op("vector", lambda yb=ybf[t % 2]: nc.vector.tensor_scalar_mul(
                        yb[:], lnsc[:], st_r[:]), rd=("lnsc", "st_r"), wr=(yk,))
                    for cch in range(DC):
                        ctx.op("tensor", lambda yb=ybf[t % 2], cch=cch: nc.tensor.transpose(
                            ps_tr[:], yb[:, cch * 128:(cch + 1) * 128], ident[:]),
                            rd=(yk, "ident"), wr=("ps_tr",))
                        ctx.op("vector", lambda cch=cch, t=t: nc.vector.tensor_copy(
                            yT[:, cch, t * 128:(t + 1) * 128], ps_tr[:]),
                            rd=("ps_tr",), wr=("yT",))
            ctx.barrier(omit=("q_gpsimd",))

            with contextlib.ExitStack() as p3:
                def a3(name, shape, dt):
                    return p3.enter_context(nc.sbuf_tensor(tag + name, shape, dt))
                wqs = [a3(f"wqs{i}", [128, DC, 128], BF16) for i in range(2)]
                wvs = [a3(f"wvs{i}", [128, DC, 512], BF16) for i in range(2)]
                ev = [a3(f"ev{i}", [128, 512], BF16) for i in range(2)]
                ps_p = [p3.enter_context(
                    nc.psum_tensor(tag + f"psp{i}", [128, 512], F32)) for i in range(4)]

                for ofc in range(16):
                    wk = f"wqs{ofc % 2}"
                    ctx.dma("sync", lambda ofc=ofc, b=wqs[ofc % 2]: nc.sync.dma_start(
                        out=b[:], in_=wq_r[:, :, ofc * 128:(ofc + 1) * 128]),
                        rd=("wqkv",), wr=(wk,), sem=f"w_g{ofc % 2}")
                    dst_h = qfm_hbm if ofc < 8 else kfm_hbm
                    cchunk = ofc % 8
                    items = []
                    for kd in range(DC):
                        for tb in range(4):
                            items.append((lambda kd=kd, ofc=ofc, tb=tb: nc.tensor.matmul(
                                ps_p[tb][:], wqs[ofc % 2][:, kd, :],
                                yT[:, kd, tb * 512:(tb + 1) * 512],
                                start=(kd == 0), stop=(kd == DC - 1)), (wk, "yT"),
                                f"ps_p{tb}", kd == 0, kd == DC - 1))
                    ctx.mms(items)
                    for tb in range(4):
                        pk = f"ps_p{tb}"
                        ek = f"ev{tb % 2}"
                        if ofc < 8:
                            ctx.op("scalar", lambda tb=tb: nc.scalar.mul(
                                ev[tb % 2][:], ps_p[tb][:], 0.125), rd=(pk,), wr=(ek,))
                        else:
                            ctx.op("scalar", lambda tb=tb: nc.scalar.copy(
                                ev[tb % 2][:], ps_p[tb][:]), rd=(pk,), wr=(ek,))
                        ctx.dma("sync", lambda tb=tb, cchunk=cchunk, dst_h=dst_h,
                            b=ev[tb % 2]: nc.sync.dma_start(
                                out=dst_h.ap()[:, cchunk, tb * 512:(tb + 1) * 512],
                                in_=b[:]), rd=(ek,), wr=(f"spl{ofc}_{tb}",),
                            sem=f"s_sp{tb % 2}")
                for vh in range(2):
                    wk = f"wvs{vh}"
                    ctx.dma("sync", lambda vh=vh, b=wvs[vh]: nc.sync.dma_start(
                        out=b[:], in_=wq_r[:, :, 2 * D + vh * 512:2 * D + (vh + 1) * 512]),
                        rd=("wqkv",), wr=(wk,), sem=f"w_u{vh}")
                    for t in range(NT):
                        pk = f"ps_p{t % 2}"
                        parts = [(lambda kd=kd, vh=vh, t=t: nc.tensor.matmul(
                            ps_p[t % 2][:], yT[:, kd, t * 128:(t + 1) * 128],
                            wvs[vh][:, kd, :],
                            start=(kd == 0), stop=(kd == DC - 1)), (wk, "yT"))
                            for kd in range(DC)]
                        ctx.mm_group(parts, pk)
                        ek = f"ev{t % 2}"
                        ctx.op("scalar", lambda t=t: nc.scalar.copy(
                            ev[t % 2][:], ps_p[t % 2][:]), rd=(pk,), wr=(ek,))
                        ctx.dma("sync", lambda t=t, vh=vh, b=ev[t % 2]: nc.sync.dma_start(
                            out=vtm_hbm.ap()[:, t, vh * 512:(vh + 1) * 512], in_=b[:]),
                            rd=(ek,), wr=(f"spv{vh}_{t}",), sem=f"s_sp{t % 2}")
            ctx.barrier(omit=("q_gpsimd",))

            with contextlib.ExitStack() as p4:
                def a4(name, shape, dt):
                    return p4.enter_context(nc.sbuf_tensor(tag + name, shape, dt))
                poolQ = a4("poolQ", [128, DC, PACK_W], BF16)
                poolK = a4("poolK", [128, DC, PACK_W], BF16)
                gpool = a4("gpool", [128, DC, 56], BF16)
                pch = [a4(f"pch{i}", [128, 1024], F32) for i in range(2)]
                agts = a4("agts", [16, D], F32)
                agim = a4("agim", [128, D], BF16)
                halo = a4("halo", [128, DC, 32], BF16)
                khsel = a4("khsel", [32, D], BF16)
                kvsrc = a4("kvsrc", [128, DC, 48], BF16)
                qsrc = a4("qsrc", [128, DC, 40], BF16)
                wcs = [a4(f"wcs{i}", [128, DC, 128], BF16) for i in range(2)]
                wvc = [a4("wvc0", [128, DC, 512], BF16)] * 2
                ps_t32 = p4.enter_context(nc.psum_tensor(tag + "pt32", [128, 128], F32))
                ps_h = p4.enter_context(nc.psum_tensor(tag + "psh", [128, 512], F32))
                ps_ht = p4.enter_context(nc.psum_tensor(tag + "psht", [128, 128], BF16))

                ctx.op("vector", lambda: nc.vector.memset(poolQ[:], 0.0), wr=("poolQ",))
                ctx.op("vector", lambda: nc.vector.memset(poolK[:], 0.0), wr=("poolK",))

                for cch in range(DC):
                    cur = 0
                    for l in range(1, 9):
                        n = T >> l
                        if l == 1:
                            ctx.op("vector", lambda cch=cch: nc.vector.tensor_add(
                                pch[0][:, 0:1024], yT[:, cch, 0:T:2], yT[:, cch, 1:T:2]),
                                rd=("yT",), wr=("pch0",))
                            cur = 0
                        else:
                            ctx.op("vector", lambda cur=cur, n=n: nc.vector.tensor_add(
                                pch[1 - cur][:, 0:n], pch[cur][:, 0:2 * n:2],
                                pch[cur][:, 1:2 * n:2]),
                                rd=(f"pch{cur}",), wr=(f"pch{1 - cur}",))
                            cur = 1 - cur
                        src = pch[cur]
                        sk = f"pch{cur}"
                        if l in LVLS:
                            nb = n // 16
                            off = LVL_PACK_OFF[l]
                            hw = (nb // 2) * 16
                            ctx.op("vector", lambda src=src, cch=cch, off=off, n=n, hw=hw:
                                nc.vector.tensor_copy(
                                    poolQ[:, cch, off:off + hw],
                                    src[:, 0:n].rearrange("p (b x) -> p b x", x=32)[:, :, 16:32]),
                                rd=(sk,), wr=("poolQ",))
                            ctx.op("vector", lambda src=src, cch=cch, off=off, n=n, hw=hw:
                                nc.vector.tensor_copy(
                                    poolK[:, cch, off:off + hw],
                                    src[:, 0:n].rearrange("p (b x) -> p b x", x=32)[:, :, 0:16]),
                                rd=(sk,), wr=("poolK",))
                        if l >= 6:
                            off = GP_OFF[l]
                            ctx.op("vector", lambda src=src, cch=cch, off=off, n=n:
                                nc.vector.tensor_copy(
                                    gpool[:, cch, off:off + n], src[:, 0:n]),
                                rd=(sk,), wr=("gpool",))
                        if l == 7:
                            ctx.op("tensor", lambda src=src: nc.tensor.transpose(
                                ps_t32[0:16, 0:128], src[:, 0:16], ident32[:]),
                                rd=(sk, "ident32"), wr=("ps_t32",))
                            ctx.op("vector", lambda cch=cch: nc.vector.tensor_copy(
                                agts[:, cch * 128:(cch + 1) * 128], ps_t32[0:16, 0:128]),
                                rd=("ps_t32",), wr=("agts",))

                ctx.dma("sync", lambda: nc.sync.dma_start(
                    out=agin_hbm.ap(), in_=agts[:]), rd=("agts",), wr=("agin",),
                    sem="s_ag")
                ctx.cc(lambda: nc.gpsimd.collective_compute(
                    "AllGather", ALU.bypass, replica_groups=[list(range(NCORES))],
                    ins=[agin_hbm.ap().opt()], outs=[agout_hbm.ap().opt()]),
                    rd=("agin",), wr=("agout",))
                ctx.dma("gpsimd", lambda: nc.gpsimd.dma_start(
                    out=agim[:], in_=agout_hbm.ap()), rd=("agout",), wr=("agim",),
                    sem="s_ag2")
                for half in range(2):
                    ctx.mm_group([(lambda half=half: nc.tensor.matmul(
                        ps_h[0:32, 0:512], selc[:],
                        agim[:, half * 512:(half + 1) * 512],
                        start=True, stop=True), ("selc", "agim"))], "ps_h")
                    ctx.op("vector", lambda half=half: nc.vector.tensor_copy(
                        khsel[:, half * 512:(half + 1) * 512], ps_h[0:32, 0:512]),
                        rd=("ps_h",), wr=("khsel",))
                for cch in range(DC):
                    ctx.op("tensor", lambda cch=cch: nc.tensor.transpose(
                        ps_ht[0:128, 0:32], khsel[:, cch * 128:(cch + 1) * 128],
                        ident[0:32, 0:32]), rd=("khsel", "ident"), wr=("ps_ht",))
                    ctx.op("vector", lambda cch=cch: nc.vector.tensor_copy(
                        halo[:, cch, :], ps_ht[0:128, 0:32]), rd=("ps_ht",), wr=("halo",))
                ctx.op("vector", lambda: nc.vector.tensor_copy(
                    kvsrc[:, :, 0:16], gpool[:, :, 0:16]), rd=("gpool",), wr=("kvsrc",))
                ctx.op("vector", lambda: nc.vector.tensor_copy(
                    kvsrc[:, :, 16:48], halo[:]), rd=("halo",), wr=("kvsrc",))
                ctx.op("vector", lambda: nc.vector.tensor_copy(
                    qsrc[:, :, 0:40], gpool[:, :, 16:56]), rd=("gpool",), wr=("qsrc",))

                ctx.op("vector", lambda: nc.vector.memset(
                    vc[:, :, 1024:1025], 1.0), wr=("vc",))


                for ofc in range(16):
                    wk = f"wcs{ofc % 2}"
                    ctx.dma("sync", lambda ofc=ofc, b=wcs[ofc % 2]: nc.sync.dma_start(
                        out=b[:], in_=wq_r[:, :, ofc * 128:(ofc + 1) * 128]),
                        rd=("wqkv",), wr=(wk,), sem=f"w_g{ofc % 2}")
                    isq = ofc < 8
                    cchunk = ofc % 8
                    dstc, dkey = (qc, "qc") if isq else (kc, "kc")
                    srcp, spk = (poolQ, "poolQ") if isq else (poolK, "poolK")
                    for nb0 in range(3):
                        n0 = nb0 * 512
                        n1 = min(n0 + 512, PACK_W)
                        parts = [(lambda kd=kd, ofc=ofc, n0=n0, n1=n1, srcp=srcp:
                            nc.tensor.matmul(
                                ps_h[:, 0:n1 - n0], wcs[ofc % 2][:, kd, :],
                                srcp[:, kd, n0:n1],
                                start=(kd == 0), stop=(kd == DC - 1)),
                            (wk, spk)) for kd in range(DC)]
                        ctx.mm_group(parts, "ps_h")
                        ctx.op("vector", lambda cchunk=cchunk, n0=n0, n1=n1, dstc=dstc:
                            nc.vector.tensor_copy(
                                dstc[:, cchunk, n0:n1], ps_h[:, 0:n1 - n0]),
                            rd=("ps_h",), wr=(dkey,))
                    srcg, sgk = (qsrc, "qsrc") if isq else (kvsrc, "kvsrc")
                    ng = 40 if isq else 48
                    dstg, dgk = (qg, "qg") if isq else (kg, "kg")
                    parts = [(lambda kd=kd, ofc=ofc, srcg=srcg, ng=ng:
                        nc.tensor.matmul(
                            ps_h[:, 0:ng], wcs[ofc % 2][:, kd, :],
                            srcg[:, kd, 0:ng],
                            start=(kd == 0), stop=(kd == DC - 1)),
                        (wk, sgk)) for kd in range(DC)]
                    ctx.mm_group(parts, "ps_h")
                    ctx.op("vector", lambda cchunk=cchunk, ng=ng, dstg=dstg, dgk=dgk:
                        nc.vector.tensor_copy(dstg[:, cchunk, 0:ng], ps_h[:, 0:ng]),
                        rd=("ps_h",), wr=(dgk,))
                for vh in range(2):
                    wk = "wvc0"
                    ctx.dma("sync", lambda vh=vh, b=wvc[vh]: nc.sync.dma_start(
                        out=b[:], in_=wq_r[:, :, 2 * D + vh * 512:2 * D + (vh + 1) * 512]),
                        rd=("wqkv",), wr=(wk,), sem=f"w_u{vh}")
                    for cb in range(9):
                        parts = [(lambda kd=kd, vh=vh, cb=cb: nc.tensor.matmul(
                            ps_h[:], poolK[:, kd, cb * 128:(cb + 1) * 128],
                            wvc[vh][:, kd, :],
                            start=(kd == 0), stop=(kd == DC - 1)),
                            (wk, "poolK")) for kd in range(DC)]
                        ctx.mm_group(parts, "ps_h")
                        ctx.op("vector", lambda cb=cb, vh=vh: nc.vector.tensor_copy(
                            vc[:, cb, vh * 512:(vh + 1) * 512], ps_h[:]),
                            rd=("ps_h",), wr=("vc",))
                    for lv in range(3):
                        parts = [(lambda kd=kd, vh=vh, lv=lv: nc.tensor.matmul(
                            ps_h[0:16, :], kvsrc[:, kd, lv * 16:(lv + 1) * 16],
                            wvc[vh][:, kd, :],
                            start=(kd == 0), stop=(kd == DC - 1)),
                            (wk, "kvsrc")) for kd in range(DC)]
                        ctx.mm_group(parts, "ps_h")
                        ctx.op("vector", lambda vh=vh, lv=lv: nc.vector.tensor_copy(
                            vg[:, lv, vh * 512:(vh + 1) * 512], ps_h[0:16, :]),
                            rd=("ps_h",), wr=("vg",))
            ctx.barrier(omit=("q_gpsimd",))

        with contextlib.ExitStack() as pb:
            def bl(name, shape, dt):
                return pb.enter_context(nc.sbuf_tensor(tag + name, shape, dt))
            scc_sb = bl("scc", [128, 9, T], BF16)
            sc678_sb = bl("s678", [16, NT, 3, 128], BF16)
            ctx.dma("sync", lambda: nc.sync.dma_start(
                out=scc_sb[:], in_=scc_in.rearrange("(g p) t -> p g t", p=128)),
                wr=("scc",), sem="s_const")
            ctx.dma("sync", lambda: nc.sync.dma_start(
                out=sc678_sb[:], in_=sc678_in.rearrange("t l q f -> q t l f")),
                wr=("sc678",), sem="s_const")
            ctx.settle(("scc", "sc678"), ("s_const",))
            q_h = [bl(f"qh{i}", [64, T], BF16) for i in range(2)]
            k_h = [bl(f"kh{i}", [64, T], BF16) for i in range(2)]
            v_h = [bl(f"vh{i}", [128, NT, 65], BF16) for i in range(2)]
            Ycomp = [bl(f"Ycomp{i}", [128, 9, 65], BF16) for i in range(2)]
            Y678 = [bl(f"Y678{i}", [16, 3, 65], BF16) for i in range(2)]
            At_sb = [bl(f"At{i}", [128, 128], BF16) for i in range(2)]
            sm_c = [bl(f"smc{i}", [128, 128], F32) for i in range(2)]
            Ab = [bl(f"Ab{i}", [128, 128], BF16) for i in range(2)]
            mx = [bl(f"mx{i}", [128, 1], F32) for i in range(2)]
            bias = [bl(f"bias{i}", [128, 1], F32) for i in range(2)]
            vy = [bl(f"vy{i}", [128, 65], BF16) for i in range(2)]
            nrm = [bl(f"nrm{i}", [128, 64], BF16) for i in range(2)]
            avs = [bl(f"avs{i}", [128, 1], F32) for i in range(2)]
            ps_sc = [pb.enter_context(nc.psum_tensor(tag + f"pssc{i}", [128, 128], F32))
                     for i in range(2)]
            ps_at = [pb.enter_context(nc.psum_tensor(tag + f"psat{i}", [128, 128], BF16))
                     for i in range(2)]
            ps_y = [pb.enter_context(nc.psum_tensor(tag + f"psy{i}", [128, 65], F32))
                    for i in range(2)]
            ps_nt = [pb.enter_context(nc.psum_tensor(tag + f"psnt{i}", [128, 128], BF16))
                     for i in range(2)]

            for i in range(2):
                ctx.op("vector", lambda i=i: nc.vector.memset(
                    v_h[i][:, :, 64:65], 1.0), wr=(f"vhones{i}",))
            for i in range(2):
                ctx.op("vector", lambda i=i: nc.vector.memset(
                    vy[i][:, 64:65], 1.0), wr=(f"vy{i}",))

            for h in range(HEADS):
                hrow = (h % 2) * 64
                hch = h // 2
                qk, kk, vk = f"q_h{h % 2}", f"k_h{h % 2}", f"v_h{h % 2}"
                ctx.dma("sync", lambda h=h, b=q_h[h % 2], hrow=hrow, hch=hch:
                    nc.sync.dma_start(out=b[:],
                        in_=qfm_hbm.ap()[hrow:hrow + 64, hch, :]),
                    rd=tuple(f"spl{hch}_{tb}" for tb in range(4)), wr=(qk,),
                    sem=f"s_qh{h % 2}")
                ctx.dma("sync", lambda h=h, b=k_h[h % 2], hrow=hrow, hch=hch:
                    nc.sync.dma_start(out=b[:],
                        in_=kfm_hbm.ap()[hrow:hrow + 64, hch, :]),
                    rd=tuple(f"spl{8 + hch}_{tb}" for tb in range(4)), wr=(kk,),
                    sem=f"s_kh{h % 2}")
                ctx.dma("sync", lambda h=h, b=v_h[h % 2]: nc.sync.dma_start(
                    out=b[:, :, 0:64], in_=vtm_hbm.ap()[:, :, h * 64:(h + 1) * 64]),
                    rd=tuple(f"spv{vh}_{t}" for vh in range(2) for t in range(0, NT, 5))
                    + (f"vhones{h % 2}",),
                    wr=(vk,), sem=f"s_vh{h % 2}")

                hh = h % 2
                yck, y6k = f"Ycomp{hh}", f"Y678{hh}"
                for g in range(9):
                    p = g % 2
                    cl = CL[GROUP_LVL[g]]
                    ctx.mm_group([(lambda g=g, p=p, hrow=hrow, hch=hch: nc.tensor.matmul(
                        ps_sc[p][:], qc[hrow:hrow + 64, hch, g * 128:(g + 1) * 128],
                        kc[hrow:hrow + 64, hch, g * 128:(g + 1) * 128],
                        start=True, stop=True), ("qc", "kc"))], f"ps_sc{p}")
                    ctx.op("vector", lambda p=p: nc.vector.tensor_add(
                        sm_c[p][:], ps_sc[p][:], mcc[:]),
                        rd=(f"ps_sc{p}", "mcc"), wr=(f"sm_c{p}",))
                    ctx.op("vector", lambda p=p: nc.vector.reduce_max(
                        mx[p][:], sm_c[p][:], axis=mybir.AxisListType.X),
                        rd=(f"sm_c{p}",), wr=(f"mx{p}",))
                    ctx.op("vector", lambda cl=cl, p=p: nc.vector.tensor_scalar_mul(
                        bias[p][:], mx[p][:], -cl), rd=(f"mx{p}",), wr=(f"bias{p}",))
                    ctx.op("scalar", lambda cl=cl, p=p: nc.scalar.activation(
                        out=Ab[p][:], in_=sm_c[p][:], func=AF.Exp, bias=bias[p][:],
                        scale=cl), rd=(f"sm_c{p}", f"bias{p}"), wr=(f"Ab{p}",))
                    ctx.op("tensor", lambda p=p: nc.tensor.transpose(
                        ps_at[p][:], Ab[p][:], ident[:]),
                        rd=(f"Ab{p}", "ident"), wr=(f"ps_at{p}",))
                    ctx.op("scalar", lambda p=p: nc.scalar.copy(
                        At_sb[p][:], ps_at[p][:]), rd=(f"ps_at{p}",), wr=(f"At_sb{p}",))
                    ctx.op("vector", lambda g=g, h=h, p=p: nc.vector.tensor_copy(
                        vy[p][:, 0:64], vc[:, g, h * 64:(h + 1) * 64]),
                        rd=("vc",), wr=(f"vy{p}",))
                    ctx.mm_group([(lambda p=p: nc.tensor.matmul(
                        ps_y[p][:], At_sb[p][:], vy[p][:], start=True, stop=True),
                        (f"At_sb{p}", f"vy{p}"))], f"ps_y{p}")
                    ctx.op("scalar", lambda g=g, p=p, yc=Ycomp[hh]: nc.scalar.copy(
                        yc[:, g, :], ps_y[p][:]), rd=(f"ps_y{p}",), wr=(yck,))

                for lvi, (lv, q0, nq, nk) in enumerate(
                        ((6, 0, 16, 16), (7, 16, 16, 16), (8, 32, 8, 16))):
                    cl = CL[lv]
                    k0 = lvi * 16
                    p = lvi % 2
                    ctx.mm_group([(lambda q0=q0, nq=nq, k0=k0, nk=nk, hrow=hrow, p=p,
                        hch=hch: nc.tensor.matmul(
                            ps_sc[p][0:nq, 0:nk], qg[hrow:hrow + 64, hch, q0:q0 + nq],
                            kg[hrow:hrow + 64, hch, k0:k0 + nk],
                            start=True, stop=True), ("qg", "kg"))], f"ps_sc{p}")
                    ctx.op("vector", lambda nq=nq, nk=nk, p=p: nc.vector.tensor_copy(
                        sm_c[p][0:nq, 0:nk], ps_sc[p][0:nq, 0:nk]),
                        rd=(f"ps_sc{p}",), wr=(f"sm_c{p}",))
                    ctx.op("vector", lambda nq=nq, nk=nk, p=p: nc.vector.reduce_max(
                        mx[p][0:nq, :], sm_c[p][0:nq, 0:nk], axis=mybir.AxisListType.X),
                        rd=(f"sm_c{p}",), wr=(f"mx{p}",))
                    ctx.op("vector", lambda cl=cl, nq=nq, p=p: nc.vector.tensor_scalar_mul(
                        bias[p][0:nq, :], mx[p][0:nq, :], -cl),
                        rd=(f"mx{p}",), wr=(f"bias{p}",))
                    ctx.op("scalar", lambda cl=cl, nq=nq, nk=nk, p=p: nc.scalar.activation(
                        out=Ab[p][0:nq, 0:nk], in_=sm_c[p][0:nq, 0:nk], func=AF.Exp,
                        bias=bias[p][0:nq, :], scale=cl),
                        rd=(f"sm_c{p}", f"bias{p}"), wr=(f"Ab{p}",))
                    ctx.op("tensor", lambda nq=nq, nk=nk, p=p: nc.tensor.transpose(
                        ps_at[p][0:nk, 0:nq], Ab[p][0:nq, 0:nk], ident[0:nq, 0:nq]),
                        rd=(f"Ab{p}", "ident"), wr=(f"ps_at{p}",))
                    ctx.op("scalar", lambda nq=nq, nk=nk, p=p: nc.scalar.copy(
                        At_sb[p][0:nk, 0:nq], ps_at[p][0:nk, 0:nq]),
                        rd=(f"ps_at{p}",), wr=(f"At_sb{p}",))
                    parts = [(lambda nq=nq, nk=nk, lvi=lvi, h=h, p=p: nc.tensor.matmul(
                        ps_y[p][0:nq, 0:64], At_sb[p][0:nk, 0:nq],
                        vg[:, lvi, h * 64:(h + 1) * 64],
                        start=True, stop=False), (f"At_sb{p}", "vg")),
                        (lambda nq=nq, nk=nk, p=p: nc.tensor.matmul(
                            ps_y[p][0:nq, 64:65], At_sb[p][0:nk, 0:nq], onescol[0:16, :],
                            start=False, stop=True), (f"At_sb{p}", "onescol"))]
                    ctx.mm_group(parts, f"ps_y{p}")
                    ctx.op("scalar", lambda lvi=lvi, nq=nq, p=p, y6=Y678[hh]:
                        nc.scalar.copy(y6[0:nq, lvi, :], ps_y[p][0:nq, 0:65]),
                        rd=(f"ps_y{p}",), wr=(y6k,))

                for t in range(NT):
                    p = t % 2
                    ctx.mm_group([(lambda t=t, h=h, p=p: nc.tensor.matmul(
                        ps_sc[p][:], q_h[h % 2][:, t * 128:(t + 1) * 128],
                        k_h[h % 2][:, t * 128:(t + 1) * 128],
                        start=True, stop=True), (qk, kk))], f"ps_sc{p}")
                    ctx.op("vector", lambda p=p: nc.vector.tensor_add(
                        sm_c[p][:], ps_sc[p][:], m0c[:]),
                        rd=(f"ps_sc{p}", "m0c"), wr=(f"sm_c{p}",))
                    ctx.op("vector", lambda p=p: nc.vector.reduce_max(
                        mx[p][:], sm_c[p][:], axis=mybir.AxisListType.X),
                        rd=(f"sm_c{p}",), wr=(f"mx{p}",))
                    ctx.op("vector", lambda p=p: nc.vector.tensor_scalar_mul(
                        bias[p][:], mx[p][:], -1.0), rd=(f"mx{p}",), wr=(f"bias{p}",))
                    ctx.op("scalar", lambda p=p: nc.scalar.activation(
                        out=Ab[p][:], in_=sm_c[p][:], func=AF.Exp, bias=bias[p][:],
                        scale=1.0), rd=(f"sm_c{p}", f"bias{p}"), wr=(f"Ab{p}",))
                    ctx.op("tensor", lambda p=p: nc.tensor.transpose(
                        ps_at[p][:], Ab[p][:], ident[:]),
                        rd=(f"Ab{p}", "ident"), wr=(f"ps_at{p}",))
                    ctx.op("scalar", lambda p=p: nc.scalar.copy(
                        At_sb[p][:], ps_at[p][:]), rd=(f"ps_at{p}",), wr=(f"At_sb{p}",))
                    parts = [(lambda t=t, h=h, p=p: nc.tensor.matmul(
                        ps_y[p][:], At_sb[p][:], v_h[h % 2][:, t, :],
                        start=True, stop=False), (f"At_sb{p}", vk))]
                    for g in OV[t]:
                        parts.append((lambda g=g, t=t, p=p, yc=Ycomp[hh]:
                            nc.tensor.matmul(
                                ps_y[p][:], scc_sb[:, g, t * 128:(t + 1) * 128],
                                yc[:, g, :], start=False, stop=False),
                            ("scc", yck)))
                    for lvi in range(3):
                        parts.append((lambda t=t, lvi=lvi, p=p, y6=Y678[hh]:
                            nc.tensor.matmul(
                                ps_y[p][:], sc678_sb[:, t, lvi, :], y6[:, lvi, :],
                                start=False, stop=(lvi == 2)), ("sc678", y6k)))
                    ctx.mm_group(parts, f"ps_y{p}")
                    ctx.op("vector", lambda p=p: nc.vector.tensor_scalar(
                        out=avs[p][:], in0=ps_y[p][:, 64:65], scalar1=EPS_ATT,
                        scalar2=None, op0=ALU.add), rd=(f"ps_y{p}",), wr=(f"avs{p}",))
                    ctx.op("vector", lambda p=p: nc.vector.reciprocal(
                        avs[p][:], avs[p][:]), rd=(f"avs{p}",), wr=(f"avs{p}",))
                    ctx.op("vector", lambda p=p: nc.vector.tensor_scalar_mul(
                        nrm[p][:], ps_y[p][:, 0:64], avs[p][:]),
                        rd=(f"ps_y{p}", f"avs{p}"), wr=(f"nrm{p}",))
                    ctx.op("tensor", lambda p=p: nc.tensor.transpose(
                        ps_nt[p][0:64, 0:128], nrm[p][:], ident[:]),
                        rd=(f"nrm{p}", "ident"), wr=(f"ps_nt{p}",))
                    ctx.op("scalar", lambda t=t, hrow=hrow, hch=hch, p=p:
                        nc.scalar.copy(
                            normT[hrow:hrow + 64, hch, t * 128:(t + 1) * 128],
                            ps_nt[p][0:64, 0:128]), rd=(f"ps_nt{p}",), wr=("normT",))
            ctx.barrier(omit=("q_gpsimd",))

        with contextlib.ExitStack() as pc:
            def cl_(name, shape, dt):
                return pc.enter_context(nc.sbuf_tensor(tag + name, shape, dt))
            wos = cl_("wos", [128, DC, D], BF16)
            xt2 = [cl_(f"xo{i}", [128, D], F32) for i in range(2)]
            x3t = [cl_(f"x3{i}", [128, D], F32) for i in range(2)]
            ps_o = [pc.enter_context(
                nc.psum_tensor(tag + f"pso{i}", [128, 512], F32)) for i in range(2)]
            ctx.dma("sync", lambda: nc.sync.dma_start(
                out=wos[:], in_=wo_r[:]), rd=("wout",), wr=("wos",), sem="s_wd")
            x3_r = (out if x_dst_is_out else x3_hbm.ap()).rearrange(
                "(t p) d -> p t d", p=128)
            for t in range(NT):
                xk = f"xo{t % 2}"
                ctx.dma("sync", lambda t=t, b=xt2[t % 2]: nc.sync.dma_start(
                    out=b[:], in_=x2_r[:, t, :]), rd=("x_dst",), wr=(xk,),
                    sem=f"s_xt{t % 2}")
                for half in range(2):
                    pok = f"ps_o{half}"
                    parts = [(lambda kd=kd, t=t, half=half: nc.tensor.matmul(
                        ps_o[half][:], normT[:, kd, t * 128:(t + 1) * 128],
                        wos[:, kd, half * 512:(half + 1) * 512],
                        start=(kd == 0), stop=False), ("normT", "wos"))
                        for kd in range(DC)]
                    parts.append((lambda half=half: nc.tensor.matmul(
                        ps_o[half][:], onesb[:],
                        bout_sb[:, half * 512:(half + 1) * 512],
                        start=False, stop=True), ("onesb", "bout")))
                    ctx.mm_group(parts, pok)
                    ctx.op("vector", lambda t=t, half=half: nc.vector.tensor_add(
                        x3t[t % 2][:, half * 512:(half + 1) * 512], ps_o[half][:],
                        xt2[t % 2][:, half * 512:(half + 1) * 512]),
                        rd=(pok, xk), wr=(f"x3{t % 2}",))
                ctx.dma("sync", lambda t=t, b=x3t[t % 2]: nc.sync.dma_start(
                    out=x3_r[:, t, :], in_=b[:]), rd=(f"x3{t % 2}",),
                    wr=("x_dst3",), sem=f"s_x3{t % 2}")
        ctx.barrier()



def kernel(L0_attn_x, L0_attn_w_qkv, L0_attn_w_out, L0_attn_b_out,
           L0_ffn_w_queries, L0_ffn_keys, L0_ffn_down_embed, L0_ffn_up_embed,
           L0_ffn_w_gate, L0_ffn_w_up, L0_ffn_w_down, _stages=3):
    x = np.ascontiguousarray(np.asarray(L0_attn_x))
    assert x.shape == (2, 8192, D)

    wqkvT = np.ascontiguousarray(np.asarray(L0_attn_w_qkv).T)
    woutT = np.ascontiguousarray(np.asarray(L0_attn_w_out).T)
    bout = np.ascontiguousarray(np.asarray(L0_attn_b_out).reshape(1, D))
    wqT = np.ascontiguousarray(np.asarray(L0_ffn_w_queries).T)
    keys = np.asarray(L0_ffn_keys)
    keysT = np.ascontiguousarray(keys.transpose(0, 3, 2, 1).reshape(128, 2, 128))
    tbl = np.ascontiguousarray(np.concatenate(
        [np.asarray(L0_ffn_down_embed), np.asarray(L0_ffn_up_embed)], axis=1))
    wgT = np.ascontiguousarray(np.asarray(L0_ffn_w_gate).T)
    wuT = np.ascontiguousarray(np.asarray(L0_ffn_w_up).T)
    wdT = np.ascontiguousarray(np.asarray(L0_ffn_w_down).T)

    in_maps = []
    metas = []
    for cid in range(NCORES):
        b, c = divmod(cid, 4)
        cc = _host_consts(cid)
        metas.append(cc)
        in_maps.append({
            "x_in": np.ascontiguousarray(x[b, c * T:(c + 1) * T]),
            "wqkvT": wqkvT, "woutT": woutT, "bout": bout, "wqT": wqT,
            "keysT": keysT, "tbl_in": tbl, "wgT": wgT, "wuT": wuT, "wdT": wdT,
            "ident": cc["ident"], "ident32": np.eye(128, dtype=np.float32),
            "m0c": cc["M0"], "mcc": cc["MC"],
            "iota64": cc["iota64"], "ones_bf": cc["ones_bf"],
            "scc": cc["ScC"], "sc678": cc["Sc678"], "sel": cc["Sel"],
            "ones_col": np.ones((128, 1), dtype=ml_dtypes.bfloat16),
        })

    nc = build_nc(metas[0], stages=_stages)
    res = run_bass_kernel_spmd(nc, in_maps, core_ids=list(range(NCORES)))
    outp = np.empty((2, 8192, D), np.float32)
    for cid in range(NCORES):
        b, c = divmod(cid, 4)
        outp[b, c * T:(c + 1) * T] = res.results[cid]["out"]
    if DEBUG:
        dbg = {k: [res.results[cid].get("dbg_" + k) for cid in range(NCORES)]
               for k in DEBUG}
        return outp, dbg
    return outp


# version 06 (final)
# speedup vs baseline: 1.0096x; 1.0096x over previous
import contextlib
import numpy as np
import ml_dtypes

import concourse.bass_utils as _bu

if not getattr(_bu, "_dge_patched", False):
    _orig_walrus_args = _bu.get_walrus_args

    def _patched_walrus_args(arch, tmpdir, **kw):
        return _orig_walrus_args(arch, tmpdir, **kw) + [
            "--dge-levels=io,spill_reload,scalar_dynamic_offset,"
            "vector_dynamic_offsets,dynamic_size,dst_reduce"
        ]

    _bu.get_walrus_args = _patched_walrus_args
    _bu._dge_patched = True

import concourse.bass as bass
import concourse.mybir as mybir
from concourse.bass_utils import run_bass_kernel_spmd
from concourse.bass_core_rust import IndirectOffsetOnAxis

F32 = mybir.dt.float32
BF16 = mybir.dt.bfloat16
I32 = mybir.dt.int32
U32 = mybir.dt.uint32
AF = mybir.ActivationFunctionType
ALU = mybir.AluOpType

NCORES = 8
T = 2048
NT = 16
D = 1024
DC = 8
INTER = 2048
NM = 16
E = 16384
EPS_LN = 1e-5
EPS_ATT = 1e-8
NEG = -1.0e9

LVLS = [1, 2, 3, 4, 5]
LVL_NCOARSE = {l: T >> l for l in range(1, 9)}
LVL_PACK_OFF = {1: 0, 2: 512, 3: 768, 4: 896, 5: 1024}
LVL_PACK_N = {1: 512, 2: 256, 3: 128, 4: 64, 5: 32}
PACK_W = 1152
GROUP_LVL = [1, 1, 1, 1, 2, 2, 3, 4, 5]
CL = {l: 0.125 * (0.25 ** l) for l in range(0, 9)}
GP_OFF = {6: 0, 7: 32, 8: 48}
HEADS = 16
DH = 64

ENGINES = ("tensor", "vector", "scalar", "sync", "gpsimd")
DEBUG = []
_DBG_TENSORS = {}


class Ctx:

    def __init__(self, nc):
        self.nc = nc
        self.streams = {e: [] for e in ENGINES}
        self.cnt = {}
        self.wm = {e: {} for e in ENGINES}
        self.wr = {}
        self.rd = {}
        for e in ENGINES:
            self.cnt["e_" + e] = 0
        for q in ("sync", "gpsimd"):
            self.cnt["q_" + q] = 0
        self.cnt["cc"] = 0

    def _wait(self, eng, sem, val):
        if self.wm[eng].get(sem, 0) >= val:
            return
        self.wm[eng][sem] = val
        self.streams[eng].append(("wait", sem, val))

    def _dep(self, eng, rd_keys, wr_keys):
        need = {}
        for k in rd_keys:
            for s, v in self.wr.get(k, ()):
                need[s] = max(need.get(s, 0), v)
        for k in wr_keys:
            for s, v in self.wr.get(k, ()):
                need[s] = max(need.get(s, 0), v)
            for s, v in self.rd.get(k, {}).items():
                need[s] = max(need.get(s, 0), v)
        for s, v in need.items():
            self._wait(eng, s, v)

    def _rec(self, rd_keys, wr_keys, sem, val):
        for k in rd_keys:
            self.rd.setdefault(k, {})[sem] = val
        for k in wr_keys:
            self.wr[k] = [(sem, val)]
            self.rd[k] = {}

    def settle(self, keys, sems_):
        for k in keys:
            self.wr[k] = [(s, self.cnt[s]) for s in sems_ if self.cnt.get(s, 0) > 0]
            self.rd[k] = {}

    def op(self, eng, fn, rd=(), wr=()):
        self._dep(eng, rd, wr)
        sem = "e_" + eng
        self.cnt[sem] += 1
        self.streams[eng].append(("op", fn, sem, 1))
        self._rec(rd, wr, sem, self.cnt[sem])

    def dma(self, eng, fn, rd=(), wr=(), sem=None):
        self._dep(eng, rd, wr)
        sem = sem or ("q_" + eng)
        self.cnt[sem] = self.cnt.get(sem, 0) + 16
        self.streams[eng].append(("op", fn, sem, 16))
        self._rec(rd, wr, sem, self.cnt[sem])

    def cc(self, fn, rd=(), wr=()):
        self._dep("gpsimd", rd, wr)
        self.cnt["cc"] += 1
        self.streams["gpsimd"].append(("op", fn, "cc", 1))
        self._rec(rd, wr, "cc", self.cnt["cc"])

    def mm_group(self, parts, psum_key):
        sem = "e_tensor"
        n = len(parts)
        for i, (fn, rds) in enumerate(parts):
            self._dep("tensor", rds, (psum_key,) if i == 0 else ())
            self.cnt[sem] += 1
            self.streams["tensor"].append(("op", fn, sem, 1))
            self._rec(rds, (psum_key,) if i == n - 1 else (), sem, self.cnt[sem])

    def barrier(self, omit=()):
        for e in ENGINES:
            for s in sorted(self.cnt):
                if s in omit:
                    continue
                if self.cnt[s] > 0:
                    self._wait(e, s, self.cnt[s])
        self.wr = {}
        self.rd = {}

    def emit(self, sems, block):
        def mk(eng):
            def body(_obj):
                for item in self.streams[eng]:
                    if item[0] == "wait":
                        _, s, v = item
                        getattr(self.nc, eng).wait_ge(sems[s], v)
                    else:
                        _, fn, s, inc = item
                        fn().then_inc(sems[s], inc)
            return body
        for eng in ENGINES:
            if self.streams[eng]:
                getattr(block, eng)(mk(eng))


def _host_consts(core_id):
    b, c = divmod(core_id, 4)
    ident = np.eye(128, dtype=ml_dtypes.bfloat16)

    M0 = np.full((128, 128), NEG, np.float32)
    for i in range(128):
        blk = i // 16
        M0[i, blk * 16: blk * 16 + (i % 16) + 1] = 0.0
    MC = np.full((128, 128), NEG, np.float32)
    for i in range(128):
        blk = i // 16
        MC[i, blk * 16:(blk + 1) * 16] = 0.0

    iota64 = np.tile(np.arange(64, dtype=np.float32), (128, 1))
    ones_bf = np.ones((1, 128), dtype=ml_dtypes.bfloat16)

    ScC = np.zeros((PACK_W, T), np.float32)
    for l in LVLS:
        ncoarse = LVL_NCOARSE[l]
        nblk = ncoarse // 16
        oddpos = []
        for blk in range(nblk):
            if blk % 2 == 1:
                oddpos.extend(range(blk * 16, blk * 16 + 16))
        for j, cpos in enumerate(oddpos):
            fine = cpos * (1 << l) + (1 << l) - 1
            ScC[LVL_PACK_OFF[l] + j, fine] = 1.0
    ov = []
    for g in range(9):
        for t in range(NT):
            if ScC[g * 128:(g + 1) * 128, t * 128:(t + 1) * 128].any():
                ov.append((g, t))
    ScC_bf = ScC.astype(ml_dtypes.bfloat16)

    Sc678 = np.zeros((NT, 3, 16, 128), np.float32)
    for i in range(16):
        fine = (16 + i) * 64 + 63
        Sc678[fine // 128, 0, i, fine % 128] = 1.0
    if c % 2 == 1:
        for i in range(16):
            fine = i * 128 + 127
            Sc678[fine // 128, 1, i, fine % 128] = 1.0
    if c >= 2:
        for i in range(8):
            fine = i * 256 + 255
            Sc678[fine // 128, 2, i, fine % 128] = 1.0
    o678 = [t for t in range(NT) if Sc678[t].any()]
    Sc678_bf = Sc678.astype(ml_dtypes.bfloat16)

    Sel = np.zeros((128, 32), np.float32)
    if c % 2 == 1:
        src = (4 * b + (c - 1)) * 16
        for i in range(16):
            Sel[src + i, i] = 1.0
    if c >= 2:
        for j in range(16):
            chunk = j // 8
            i = j % 8
            src = (4 * b + chunk) * 16
            Sel[src + 2 * i, 16 + j] = 1.0
            Sel[src + 2 * i + 1, 16 + j] = 1.0
    Sel_bf = Sel.astype(ml_dtypes.bfloat16)

    return dict(ident=ident, M0=M0, MC=MC, iota64=iota64, ones_bf=ones_bf,
                ScC=ScC_bf, ov=ov, Sc678=Sc678_bf, o678=o678, Sel=Sel_bf)


def build_nc(meta, stages=3):
    nc = bass.Bass("TRN2", target_bir_lowering=False)
    ctx = Ctx(nc)

    x_in = nc.dram_tensor("x_in", [T, D], F32, kind="ExternalInput").ap()
    wqkvT = nc.dram_tensor("wqkvT", [D, 3 * D], F32, kind="ExternalInput").ap()
    woutT_i = nc.dram_tensor("woutT", [D, D], F32, kind="ExternalInput").ap()
    bout = nc.dram_tensor("bout", [1, D], F32, kind="ExternalInput").ap()
    wqT = nc.dram_tensor("wqT", [D, 256], F32, kind="ExternalInput").ap()
    keysT = nc.dram_tensor("keysT", [128, 2, 128], F32, kind="ExternalInput").ap()
    tbl_in = nc.dram_tensor("tbl_in", [E, 2 * D], F32, kind="ExternalInput").ap()
    wgT = nc.dram_tensor("wgT", [D, INTER], F32, kind="ExternalInput").ap()
    wuT = nc.dram_tensor("wuT", [D, INTER], F32, kind="ExternalInput").ap()
    wdT = nc.dram_tensor("wdT", [INTER, D], F32, kind="ExternalInput").ap()
    ident_in = nc.dram_tensor("ident", [128, 128], BF16, kind="ExternalInput").ap()
    ident32_in = nc.dram_tensor("ident32", [128, 128], F32, kind="ExternalInput").ap()
    m0_in = nc.dram_tensor("m0c", [128, 128], F32, kind="ExternalInput").ap()
    mc_in = nc.dram_tensor("mcc", [128, 128], F32, kind="ExternalInput").ap()
    iota_in = nc.dram_tensor("iota64", [128, 64], F32, kind="ExternalInput").ap()
    ones_in = nc.dram_tensor("ones_bf", [1, 128], BF16, kind="ExternalInput").ap()
    scc_in = nc.dram_tensor("scc", [PACK_W, T], BF16, kind="ExternalInput").ap()
    sc678_in = nc.dram_tensor("sc678", [NT, 3, 16, 128], BF16, kind="ExternalInput").ap()
    onescol_in = nc.dram_tensor("ones_col", [128, 1], BF16, kind="ExternalInput").ap()
    sel_in = nc.dram_tensor("sel", [128, 32], BF16, kind="ExternalInput").ap()

    out = nc.dram_tensor("out", [T, D], F32, kind="ExternalOutput").ap()

    dbg_outs = {}

    def tap(name, ap_fn, shape, dt, rd):
        if name not in DEBUG:
            return
        o = nc.dram_tensor("dbg_" + name, shape, dt, kind="ExternalOutput").ap()
        dbg_outs[name] = (shape, dt)
        ctx.dma("sync", lambda: nc.sync.dma_start(out=o, in_=ap_fn()), rd=rd, wr=())

    tbl_bf = nc.dram_tensor("tbl_bf", [E, 2 * D], BF16)
    wqkvT_bf = nc.dram_tensor("wqkvT_bf", [D, 3 * D], BF16)
    woutT_bf = nc.dram_tensor("woutT_bf", [D, D], BF16)
    wgT_bf = nc.dram_tensor("wgT_bf", [D, INTER], BF16)
    wuT_bf = nc.dram_tensor("wuT_bf", [D, INTER], BF16)
    wdT_bf = nc.dram_tensor("wdT_bf", [INTER, D], BF16)
    hbf_hbm = nc.dram_tensor("hbf_hbm", [T, D], BF16)
    x2_hbm = nc.dram_tensor("x2_hbm", [T, D], F32)
    x3_hbm = nc.dram_tensor("x3_hbm", [T, D], F32)
    qfm_hbm = nc.dram_tensor("qfm_hbm", [128, DC, T], BF16)
    kfm_hbm = nc.dram_tensor("kfm_hbm", [128, DC, T], BF16)
    vtm_hbm = nc.dram_tensor("vtm_hbm", [128, NT, D], BF16)
    agin_hbm = nc.dram_tensor("agin_hbm", [16, D], F32)
    agout_hbm = nc.dram_tensor("agout_hbm", [128, D], F32)

    estack = contextlib.ExitStack()

    with estack:
        def gsb(name, shape, dt):
            return estack.enter_context(nc.sbuf_tensor(name, shape, dt))

        ident = gsb("identc", [128, 128], BF16)
        ident32 = gsb("identc32", [128, 128], F32)
        m0c = gsb("m0cs", [128, 128], F32)
        mcc = gsb("mccs", [128, 128], F32)
        iota64 = gsb("iota64s", [128, 64], F32)
        onesb = gsb("onesbs", [1, 128], BF16)
        onescol = gsb("onescols", [128, 1], BF16)
        keysT_sb = gsb("keysTs", [128, 2, 128], F32)
        bout_sb = gsb("bouts", [1, D], BF16)
        selc = gsb("selcs", [128, 32], BF16)
        castb = [gsb(f"castb{i}", [128, D], BF16) for i in range(2)]

        ctx.dma("sync", lambda: nc.sync.dma_start(out=ident[:], in_=ident_in), wr=("ident",), sem="s_const")
        ctx.dma("sync", lambda: nc.sync.dma_start(out=ident32[:], in_=ident32_in), wr=("ident32",), sem="s_const")
        ctx.dma("sync", lambda: nc.sync.dma_start(out=m0c[:], in_=m0_in), wr=("m0c",), sem="s_const")
        ctx.dma("sync", lambda: nc.sync.dma_start(out=mcc[:], in_=mc_in), wr=("mcc",), sem="s_const")
        ctx.dma("sync", lambda: nc.sync.dma_start(out=iota64[:], in_=iota_in), wr=("iota64",), sem="s_const")
        ctx.dma("sync", lambda: nc.sync.dma_start(out=onesb[:], in_=ones_in), wr=("onesb",), sem="s_const")
        ctx.dma("sync", lambda: nc.sync.dma_start(out=onescol[:], in_=onescol_in), wr=("onescol",), sem="s_const")
        ctx.dma("sync", lambda: nc.sync.dma_start(out=selc[:], in_=sel_in), wr=("selc",), sem="s_const")
        ctx.dma("sync", lambda: nc.sync.dma_start(out=keysT_sb[:], in_=keysT), wr=("keysT",), sem="s_const")
        ctx.settle(("ident", "ident32", "m0c", "mcc", "iota64", "onesb",
                    "onescol", "selc", "keysT"), ("s_const",))
        ctx.dma("gpsimd", lambda: nc.gpsimd.dma_start(out=bout_sb[:], in_=bout), wr=("bout",), sem="s_bout")

        def cast_hbm(name, src_ap, dst_t, rows, cols):
            nch = rows // 128
            ncc = (cols + D - 1) // D
            src_r = src_ap.rearrange("(c p) d -> c p d", p=128, c=nch)
            dst_r = dst_t.ap().rearrange("(c p) d -> c p d", p=128, c=nch)
            ji = 0
            for i in range(nch):
                for j in range(ncc):
                    c0, c1 = j * D, min((j + 1) * D, cols)
                    bkey = f"castb{ji % 2}"
                    bslice = castb[ji % 2][:, 0:c1 - c0]
                    ctx.dma("gpsimd", lambda i=i, bs=bslice, c0=c0, c1=c1:
                        nc.gpsimd.dma_start(out=bs, in_=src_r[i][:, c0:c1]),
                        rd=(), wr=(bkey,), sem=f"s_cb{ji % 2}")
                    ctx.dma("gpsimd", lambda i=i, bs=bslice, c0=c0, c1=c1:
                        nc.gpsimd.dma_start(out=dst_r[i][:, c0:c1], in_=bs),
                        rd=(bkey,), wr=(name,), sem=f"s_cs{ji % 2}")
                    ji += 1
            ctx.settle((name,), ("s_cs0", "s_cs1"))

        cast_hbm("tbl", tbl_in, tbl_bf, E, 2 * D)
        cast_hbm("wg", wgT, wgT_bf, D, INTER)
        cast_hbm("wu", wuT, wuT_bf, D, INTER)
        cast_hbm("wd", wdT, wdT_bf, INTER, D)
        cast_hbm("wqkv", wqkvT, wqkvT_bf, D, 3 * D)
        cast_hbm("wout", woutT_i, woutT_bf, D, D)

        def ffn_phase(tag, x_src, x_dst):
            with contextlib.ExitStack() as ph:
                def psb(name, shape, dt):
                    return ph.enter_context(nc.sbuf_tensor(tag + name, shape, dt))

                x_sb = psb("x", [128, NT, D], F32)
                wqT_sb = psb("wqTs", [128, DC, 256], F32)
                ctx.dma("sync", lambda: nc.sync.dma_start(
                    out=wqT_sb[:], in_=wqT.rearrange("(c p) m -> p c m", p=128)),
                    wr=("wqT",), sem="s_wq")
                ctx.dma("sync", lambda: nc.sync.dma_start(
                    out=x_sb[:], in_=x_src.rearrange("(t p) d -> p t d", p=128)),
                    wr=("x_sb",), sem="s_x")

                flat_i32 = psb("flat", [128, NT, 32], I32)
                wsm = psb("wsm", [128, NT * 32], F32)
                hidden = psb("hid", [128, NM, INTER], BF16)

                with contextlib.ExitStack() as ph2:
                    def psb2(name, shape, dt):
                        return ph2.enter_context(nc.sbuf_tensor(tag + name, shape, dt))

                    yT = psb2("yT", [128, DC, T], BF16)

                    with contextlib.ExitStack() as pa:
                        def pal(name, shape, dt):
                            return pa.enter_context(nc.sbuf_tensor(tag + name, shape, dt))
                        ybf = [pal(f"ybf{i}", [128, D], BF16) for i in range(2)]
                        lnsc = pal("lnsc", [128, D], F32)

                        st_m = pal("stm", [128, 1], F32)
                        st_v = pal("stv", [128, 1], F32)
                        st_r = pal("str", [128, 1], F32)
                        yt32 = pal("yt32", [128, DC, 128], F32)
                        q32t = pal("q32t", [128, 2, 128], F32)
                        sx8 = pal("sx8", [128, 8], F32)
                        sy8 = pal("sy8", [128, 8], F32)
                        ixu = pal("ixu", [128, 8], U32)
                        iyu = pal("iyu", [128, 8], U32)
                        ixf = pal("ixf", [128, 8], F32)
                        iyf = pal("iyf", [128, 8], F32)
                        cs64 = pal("cs64", [128, 64], F32)
                        ci64 = pal("ci64", [128, 64], F32)
                        vals8 = pal("vals8", [128, 8], F32)
                        posu = pal("posu", [128, 8], U32)
                        posf = pal("posf", [128, 8], F32)
                        jk64 = pal("jk64", [128, 64], F32)
                        e8 = pal("e8", [128, 8], F32)
                        ssum = pal("ssum", [128, 1], F32)
                        flat_f = pal("flatf", [128, 32], F32)
                        ps_s = [pa.enter_context(
                            nc.psum_tensor(tag + f"pss{i}", [128, 128], F32))
                            for i in range(2)]
                        ps_tr = pa.enter_context(
                            nc.psum_tensor(tag + "ptr", [128, 128], BF16))
                        ps_t32 = pa.enter_context(
                            nc.psum_tensor(tag + "ptr32", [128, 128], F32))
                        ps_q2 = pa.enter_context(
                            nc.psum_tensor(tag + "psq2", [128, 128], F32))
                        hbf_r = hbf_hbm.ap().rearrange("(t p) d -> p t d", p=128)

                        for t in range(NT):
                            yk = f"ybf{t % 2}"
                            xt = x_sb[:, t, :]
                            ctx.op("vector", lambda xt=xt: nc.vector.reduce_sum(
                                st_m[:], xt, axis=mybir.AxisListType.X),
                                rd=("x_sb",), wr=("st_m",))
                            ctx.op("vector", lambda: nc.vector.tensor_scalar_mul(
                                st_m[:], st_m[:], 1.0 / D), rd=("st_m",), wr=("st_m",))
                            ctx.op("vector", lambda xt=xt: nc.vector.tensor_scalar_sub(
                                lnsc[:], xt, st_m[:]), rd=("x_sb", "st_m"), wr=("lnsc",))
                            ctx.op("vector", lambda yb=ybf[t % 2]: nc.vector.scalar_tensor_tensor(
                                out=yb[:], in0=lnsc[:], scalar=1.0, in1=lnsc[:],
                                op0=ALU.mult, op1=ALU.mult, accum_out=st_v[:]),
                                rd=("lnsc",), wr=(yk, "st_v"))
                            ctx.op("vector", lambda: nc.vector.tensor_scalar(
                                out=st_v[:], in0=st_v[:], scalar1=1.0 / D, scalar2=EPS_LN,
                                op0=ALU.mult, op1=ALU.add), rd=("st_v",), wr=("st_v",))
                            ctx.op("vector", lambda: nc.vector.reciprocal(
                                st_v[:], st_v[:]), rd=("st_v",), wr=("st_v",))
                            ctx.op("scalar", lambda: nc.scalar.activation(
                                out=st_r[:], in_=st_v[:], func=AF.Sqrt),
                                rd=("st_v",), wr=("st_r",))
                            ctx.op("vector", lambda: nc.vector.tensor_scalar_mul(
                                lnsc[:], lnsc[:], st_r[:]), rd=("lnsc", "st_r"), wr=("lnsc",))
                            ctx.op("vector", lambda yb=ybf[t % 2]: nc.vector.tensor_copy(
                                yb[:], lnsc[:]), rd=("lnsc",), wr=(yk,))
                            ctx.dma("sync", lambda t=t, yb=ybf[t % 2]: nc.sync.dma_start(
                                out=hbf_r[:, t, :], in_=yb[:]), rd=(yk,), wr=(f"hbf{t}",),
                                sem=f"s_ys{t % 2}")
                            for cch in range(DC):
                                ctx.op("tensor", lambda yb=ybf[t % 2], cch=cch:
                                    nc.tensor.transpose(
                                        ps_tr[:], yb[:, cch * 128:(cch + 1) * 128], ident[:]),
                                    rd=(yk, "ident"), wr=("ps_tr",))
                                ctx.op("vector", lambda cch=cch, t=t: nc.vector.tensor_copy(
                                    yT[:, cch, t * 128:(t + 1) * 128], ps_tr[:]),
                                    rd=("ps_tr",), wr=("yT",))
                                ctx.op("tensor", lambda cch=cch: nc.tensor.transpose(
                                    ps_t32[:], lnsc[:, cch * 128:(cch + 1) * 128], ident32[:]),
                                    rd=("lnsc", "ident32"), wr=("ps_t32",))
                                ctx.op("vector", lambda cch=cch: nc.vector.tensor_copy(
                                    yt32[:, cch, :], ps_t32[:]),
                                    rd=("ps_t32",), wr=("yt32",))
                            for ofc in range(2):
                                parts = [(lambda kd=kd, ofc=ofc: nc.tensor.matmul(
                                    ps_q2[:], wqT_sb[:, kd, ofc * 128:(ofc + 1) * 128],
                                    yt32[:, kd, :],
                                    start=(kd == 0), stop=(kd == DC - 1)),
                                    ("wqT", "yt32")) for kd in range(DC)]
                                ctx.mm_group(parts, "ps_q2")
                                ctx.op("vector", lambda ofc=ofc: nc.vector.tensor_copy(
                                    q32t[:, ofc, :], ps_q2[:]),
                                    rd=("ps_q2",), wr=("q32t",))
                            for h in range(4):
                                base = h * 32
                                tp = (96, 0) if base == 96 else None
                                pairs = ((0, sx8, ixu, ixf), (1, sy8, iyu, iyf))
                                for p, s8, iu, if_ in pairs:
                                    pk = f"ps_s{p}"
                                    ctx.mm_group([(lambda p=p, b_=base, tp=tp:
                                        nc.tensor.matmul(
                                            ps_s[p][:],
                                            q32t[b_:b_ + 32, p, :],
                                            keysT_sb[b_:b_ + 32, p, :],
                                            start=True, stop=True, tile_position=tp),
                                        ("q32t", "keysT"))], pk)
                                    ctx.op("vector", lambda p=p, s8=s8: nc.vector.max(
                                        out=s8[:], in_=ps_s[p][:]), rd=(pk,), wr=(f"s8_{p}",))
                                    ctx.op("vector", lambda p=p, s8=s8, iu=iu:
                                        nc.vector.max_index(
                                            out=iu[:], in_max=s8[:], in_values=ps_s[p][:]),
                                        rd=(pk, f"s8_{p}"), wr=(f"iu_{p}",))
                                    ctx.op("vector", lambda iu=iu, if_=if_:
                                        nc.vector.tensor_copy(if_[:], iu[:]),
                                        rd=(f"iu_{p}",), wr=(f"if_{p}",))
                                ctx.op("vector", lambda: nc.vector.tensor_scalar_mul(
                                    ixf[:], ixf[:], 128.0), rd=("if_0",), wr=("if_0",))
                                for i in range(8):
                                    ctx.op("vector", lambda i=i: nc.vector.tensor_scalar_add(
                                        cs64[:, i * 8:(i + 1) * 8], sy8[:], sx8[:, i:i + 1]),
                                        rd=("s8_0", "s8_1"), wr=("cs64",))
                                    ctx.op("vector", lambda i=i: nc.vector.tensor_scalar_add(
                                        ci64[:, i * 8:(i + 1) * 8], iyf[:], ixf[:, i:i + 1]),
                                        rd=("if_0", "if_1"), wr=("ci64",))
                                ctx.op("vector", lambda: nc.vector.max(
                                    out=vals8[:], in_=cs64[:]), rd=("cs64",), wr=("vals8",))
                                ctx.op("vector", lambda: nc.vector.max_index(
                                    out=posu[:], in_max=vals8[:], in_values=cs64[:]),
                                    rd=("cs64", "vals8"), wr=("posu",))
                                ctx.op("vector", lambda: nc.vector.tensor_copy(
                                    posf[:], posu[:]), rd=("posu",), wr=("posf",))
                                for sx in range(8):
                                    ctx.op("vector", lambda sx=sx, h=h:
                                        nc.vector.scalar_tensor_tensor(
                                            out=jk64[:], in0=iota64[:],
                                            scalar=posf[:, sx:sx + 1], in1=ci64[:],
                                            op0=ALU.is_equal, op1=ALU.mult,
                                            accum_out=flat_f[:, h * 8 + sx:h * 8 + sx + 1]),
                                        rd=("iota64", "posf", "ci64"),
                                        wr=("jk64", "flat_f"))
                                ctx.op("vector", lambda: nc.vector.tensor_scalar_sub(
                                    e8[:], vals8[:], vals8[:, 0:1]), rd=("vals8",), wr=("e8",))
                                ctx.op("scalar", lambda: nc.scalar.activation(
                                    out=e8[:], in_=e8[:], func=AF.Exp, accum_out=ssum[:]),
                                    rd=("e8",), wr=("e8", "ssum"))
                                ctx.op("vector", lambda: nc.vector.reciprocal(
                                    ssum[:], ssum[:]), rd=("ssum",), wr=("ssum",))
                                ctx.op("vector", lambda t=t, h=h: nc.vector.tensor_scalar(
                                    out=wsm[:, t * 32 + h * 8: t * 32 + (h + 1) * 8],
                                    in0=e8[:], scalar1=ssum[:], scalar2=0.5,
                                    op0=ALU.mult, op1=ALU.mult),
                                    rd=("e8", "ssum"), wr=("wsm",))
                            ctx.op("vector", lambda t=t: nc.vector.tensor_copy(
                                flat_i32[:, t, :], flat_f[:]), rd=("flat_f",), wr=("flat_i32",))

                    if tag == "f1":
                        tap("flat", lambda: flat_i32[:], [128, NT, 32], I32, ("flat_i32",))
                        tap("wsm", lambda: wsm[:], [128, NT * 32], F32, ("wsm",))
                    ctx.barrier(omit=("q_gpsimd",))
                    with contextlib.ExitStack() as pc:
                        def pcl(name, shape, dt):
                            return pc.enter_context(nc.sbuf_tensor(tag + name, shape, dt))
                        wg_sb = [pcl(f"wg{i}", [128, DC, 128], BF16) for i in range(2)]
                        wu_sb = [pcl(f"wu{i}", [128, DC, 128], BF16) for i in range(2)]
                        sg = [pcl(f"sg{i}", [128, 512], BF16) for i in range(2)]
                        ps_g = [pc.enter_context(
                            nc.psum_tensor(tag + f"psg{i}", [128, 512], F32)) for i in range(2)]
                        ps_u = [pc.enter_context(
                            nc.psum_tensor(tag + f"psu{i}", [128, 512], F32)) for i in range(2)]
                        wg_r = wgT_bf.ap().rearrange("(c p) m -> p c m", p=128)
                        wu_r = wuT_bf.ap().rearrange("(c p) m -> p c m", p=128)

                        for m in range(NM):
                            wgk, wuk = f"wg{m % 2}", f"wu{m % 2}"
                            ctx.dma("sync", lambda m=m, b=wg_sb[m % 2]: nc.sync.dma_start(
                                out=b[:], in_=wg_r[:, :, m * 128:(m + 1) * 128]),
                                rd=("wg",), wr=(wgk,), sem=f"w_g{m % 2}")
                            ctx.dma("sync", lambda m=m, b=wu_sb[m % 2]: nc.sync.dma_start(
                                out=b[:], in_=wu_r[:, :, m * 128:(m + 1) * 128]),
                                rd=("wu",), wr=(wuk,), sem=f"w_u{m % 2}")
                            for tb in range(4):
                                pgk = f"ps_g{tb % 2}"
                                parts = [(lambda kd=kd, m=m, tb=tb: nc.tensor.matmul(
                                    ps_g[tb % 2][:], wg_sb[m % 2][:, kd, :],
                                    yT[:, kd, tb * 512:(tb + 1) * 512],
                                    start=(kd == 0), stop=(kd == DC - 1)), (wgk, "yT"))
                                    for kd in range(DC)]
                                ctx.mm_group(parts, pgk)
                                sgk = f"sg{tb % 2}"
                                ctx.op("scalar", lambda tb=tb: nc.scalar.activation(
                                    out=sg[tb % 2][:], in_=ps_g[tb % 2][:], func=AF.Silu),
                                    rd=(pgk,), wr=(sgk,))
                                puk = f"ps_u{tb % 2}"
                                parts = [(lambda kd=kd, m=m, tb=tb: nc.tensor.matmul(
                                    ps_u[tb % 2][:], wu_sb[m % 2][:, kd, :],
                                    yT[:, kd, tb * 512:(tb + 1) * 512],
                                    start=(kd == 0), stop=(kd == DC - 1)), (wuk, "yT"))
                                    for kd in range(DC)]
                                ctx.mm_group(parts, puk)
                                ctx.op("vector", lambda m=m, tb=tb: nc.vector.tensor_mul(
                                    hidden[:, m, tb * 512:(tb + 1) * 512],
                                    sg[tb % 2][:], ps_u[tb % 2][:]),
                                    rd=(sgk, puk), wr=("hidden",))

                ctx.barrier()
                with contextlib.ExitStack() as ph3:
                    wdT_sb = ph3.enter_context(
                        nc.sbuf_tensor(tag + "wdTs", [128, NM, D], BF16))
                    ps_d = [ph3.enter_context(
                        nc.psum_tensor(tag + f"psd{i}", [128, 512], F32)) for i in range(2)]
                    ctx.dma("sync", lambda: nc.sync.dma_start(
                        out=wdT_sb[:], in_=wdT_bf.ap().rearrange("(c p) m -> p c m", p=128)),
                        rd=("wd",), wr=("wdT_sb",), sem="s_wd")
                    for t in range(NT):
                        for half in range(2):
                            pdk = f"ps_d{half}"
                            parts = [(lambda m=m, t=t, half=half: nc.tensor.matmul(
                                ps_d[half][:], hidden[:, m, t * 128:(t + 1) * 128],
                                wdT_sb[:, m, half * 512:(half + 1) * 512],
                                start=(m == 0), stop=(m == NM - 1)),
                                ("hidden", "wdT_sb")) for m in range(NM)]
                            ctx.mm_group(parts, pdk)
                            ctx.op("vector", lambda t=t, half=half:
                                nc.vector.scalar_tensor_tensor(
                                    out=x_sb[:, t, half * 512:(half + 1) * 512],
                                    in0=ps_d[half][:], scalar=0.5,
                                    in1=x_sb[:, t, half * 512:(half + 1) * 512],
                                    op0=ALU.mult, op1=ALU.add),
                                rd=(pdk, "x_sb"), wr=("x_sb",))


                ctx.barrier()
                with contextlib.ExitStack() as pd:
                    def pdl(name, shape, dt):
                        return pd.enter_context(nc.sbuf_tensor(tag + name, shape, dt))
                    ew = pdl("ew", [128, NT * 32], F32)
                    pair = [pdl(f"pair{i}", [128, 2 * D], BF16) for i in range(8)]
                    hb = [pdl(f"hb{i}", [128, D], BF16) for i in range(2)]
                    dj = pdl("dj", [128, D], BF16)
                    silu8 = pdl("silu8", [128, 8], F32)
                    wfin = pdl("wfin", [128, 8], F32)
                    hbf_r = hbf_hbm.ap().rearrange("(t p) d -> p t d", p=128)

                    for t in range(NT):
                        hk = f"hb{t % 2}"
                        ctx.dma("sync", lambda t=t, b=hb[t % 2]: nc.sync.dma_start(
                            out=b[:], in_=hbf_r[:, t, :]), rd=(f"hbf{t}",), wr=(hk,),
                            sem=f"h_b{t % 2}")
                        for h in range(4):
                            for k in range(8):
                                sslot = h * 8 + k
                                pk = f"pair{sslot % 8}"
                                ctx.dma("gpsimd", lambda t=t, sslot=sslot,
                                    pb=pair[sslot % 8]: nc.gpsimd.indirect_dma_start(
                                        pb[:], None, tbl_bf.ap(),
                                        IndirectOffsetOnAxis(
                                            ap=flat_i32[:, t, sslot:sslot + 1], axis=0)),
                                    rd=("flat_i32", "tbl"), wr=(pk,),
                                    sem=f"g_p{sslot % 8}")
                                ctx.op("vector", lambda t=t, sslot=sslot,
                                    pb=pair[sslot % 8], b=hb[t % 2]:
                                    nc.vector.scalar_tensor_tensor(
                                        out=dj[:], in0=pb[:, 0:D], scalar=1.0, in1=b[:],
                                        op0=ALU.mult, op1=ALU.mult,
                                        accum_out=ew[:, t * 32 + sslot:t * 32 + sslot + 1]),
                                    rd=(pk, hk), wr=("dj", "ew"))
                            ctx.op("scalar", lambda t=t, h=h: nc.scalar.activation(
                                out=silu8[:], in_=ew[:, t * 32 + h * 8:t * 32 + h * 8 + 8],
                                func=AF.Silu), rd=("ew",), wr=("silu8",))
                            ctx.op("vector", lambda t=t, h=h: nc.vector.tensor_mul(
                                wfin[:], silu8[:],
                                wsm[:, t * 32 + h * 8:t * 32 + h * 8 + 8]),
                                rd=("silu8", "wsm"), wr=("wfin",))
                            for k in range(8):
                                sslot = h * 8 + k
                                pk = f"pair{sslot % 8}"
                                ctx.op("vector", lambda t=t, sslot=sslot, k=k,
                                    pb=pair[sslot % 8]: nc.vector.scalar_tensor_tensor(
                                        out=x_sb[:, t, :], in0=pb[:, D:2 * D],
                                        scalar=wfin[:, k:k + 1], in1=x_sb[:, t, :],
                                        op0=ALU.mult, op1=ALU.add),
                                    rd=(pk, "wfin", "x_sb"), wr=("x_sb",))

                if tag == "f1":
                    tap("hid", lambda: hidden[:], [128, NM, INTER], BF16, ("hidden",))
                    tap("xd", lambda: x_sb[:], [128, NT, D], F32, ("x_sb",))
                    tap("hbf", lambda: hbf_hbm.ap(), [T, D], BF16, ("hbf15",))
                ctx.dma("sync", lambda: nc.sync.dma_start(
                    out=x_dst.rearrange("(t p) d -> p t d", p=128), in_=x_sb[:]),
                    rd=("x_sb",), wr=("x_dst",))
            ctx.barrier()

        ffn_phase("f1", x_in, out if stages == 1 else x2_hbm.ap())

        if stages >= 2:
            attn_env = dict(
                nc=nc, ctx=ctx, meta=meta, ident=ident, ident32=ident32,
                m0c=m0c, mcc=mcc,
                onesb=onesb, onescol=onescol, bout_sb=bout_sb, selc=selc,
                x2_hbm=x2_hbm, x3_hbm=x3_hbm, out=out,
                hbf_hbm=hbf_hbm, qfm_hbm=qfm_hbm, kfm_hbm=kfm_hbm, vtm_hbm=vtm_hbm,
                agin_hbm=agin_hbm, agout_hbm=agout_hbm,
                wqkvT_bf=wqkvT_bf, woutT_bf=woutT_bf,
                scc_in=scc_in, sc678_in=sc678_in)
            build_attention(attn_env, x_dst_is_out=(stages == 2))

        if stages >= 3:
            ffn_phase("f2", x3_hbm.ap(), out)

        ctx.streams["gpsimd"].append(("wait", "q_sync", ctx.cnt["q_sync"]))
        ctx.wm["gpsimd"]["q_sync"] = ctx.cnt["q_sync"]

        sems = {}
        for sname in sorted(ctx.cnt):
            if ctx.cnt[sname] > 0:
                sems[sname] = estack.enter_context(nc.semaphore(sname))
        with nc.Block() as block:
            ctx.emit(sems, block)

    return nc



def _ov_pairs():
    import numpy as _np
    ScC = _np.zeros((PACK_W, T), _np.float32)
    for l in LVLS:
        nblk = LVL_NCOARSE[l] // 16
        oddpos = []
        for blk in range(nblk):
            if blk % 2 == 1:
                oddpos.extend(range(blk * 16, blk * 16 + 16))
        for j, cpos in enumerate(oddpos):
            ScC[LVL_PACK_OFF[l] + j, cpos * (1 << l) + (1 << l) - 1] = 1.0
    ov = {}
    for t in range(NT):
        ov[t] = [g for g in range(9)
                 if ScC[g * 128:(g + 1) * 128, t * 128:(t + 1) * 128].any()]
    return ov

OV = _ov_pairs()


def build_attention(env, x_dst_is_out):
    nc = env["nc"]; ctx = env["ctx"]
    ident = env["ident"]; ident32 = env["ident32"]
    m0c = env["m0c"]; mcc = env["mcc"]; onesb = env["onesb"]
    onescol = env["onescol"]
    bout_sb = env["bout_sb"]; selc = env["selc"]
    x2_hbm = env["x2_hbm"]; x3_hbm = env["x3_hbm"]; out = env["out"]
    qfm_hbm = env["qfm_hbm"]; kfm_hbm = env["kfm_hbm"]; vtm_hbm = env["vtm_hbm"]
    agin_hbm = env["agin_hbm"]; agout_hbm = env["agout_hbm"]
    wqkvT_bf = env["wqkvT_bf"]; woutT_bf = env["woutT_bf"]
    scc_in = env["scc_in"]; sc678_in = env["sc678_in"]
    tag = "at"

    x2_r = x2_hbm.ap().rearrange("(t p) d -> p t d", p=128)
    wq_r = wqkvT_bf.ap().rearrange("(c p) m -> p c m", p=128)
    wo_r = woutT_bf.ap().rearrange("(c p) m -> p c m", p=128)

    with contextlib.ExitStack() as pa:
        def al(name, shape, dt):
            return pa.enter_context(nc.sbuf_tensor(tag + name, shape, dt))

        qc = al("qc", [128, DC, PACK_W], BF16)
        kc = al("kc", [128, DC, PACK_W], BF16)
        vc = al("vc", [128, 9, 1025], BF16)
        qg = al("qg", [128, DC, 40], BF16)
        kg = al("kg", [128, DC, 48], BF16)
        vg = al("vg", [16, 3, 1024], BF16)
        normT = al("nT", [128, DC, T], BF16)

        with contextlib.ExitStack() as p1:
            def a1(name, shape, dt):
                return p1.enter_context(nc.sbuf_tensor(tag + name, shape, dt))
            yT = a1("yT", [128, DC, T], BF16)

            with contextlib.ExitStack() as p2:
                def a2(name, shape, dt):
                    return p2.enter_context(nc.sbuf_tensor(tag + name, shape, dt))
                xt2 = [a2(f"xt{i}", [128, D], F32) for i in range(2)]
                ybf = [a2(f"ybf{i}", [128, D], BF16) for i in range(2)]
                lnsc = a2("lnsc", [128, D], F32)
                st_m = a2("stm", [128, 1], F32)
                st_v = a2("stv", [128, 1], F32)
                st_r = a2("str", [128, 1], F32)
                ps_tr = p2.enter_context(nc.psum_tensor(tag + "ptr", [128, 128], BF16))

                for t in range(NT):
                    xk, yk = f"xt{t % 2}", f"ybf{t % 2}"
                    ctx.dma("sync", lambda t=t, b=xt2[t % 2]: nc.sync.dma_start(
                        out=b[:], in_=x2_r[:, t, :]), rd=("x_dst",), wr=(xk,),
                        sem=f"s_xt{t % 2}")
                    xt = xt2[t % 2]
                    ctx.op("vector", lambda xt=xt: nc.vector.reduce_sum(
                        st_m[:], xt[:], axis=mybir.AxisListType.X), rd=(xk,), wr=("st_m",))
                    ctx.op("vector", lambda: nc.vector.tensor_scalar_mul(
                        st_m[:], st_m[:], 1.0 / D), rd=("st_m",), wr=("st_m",))
                    ctx.op("vector", lambda xt=xt: nc.vector.tensor_scalar_sub(
                        lnsc[:], xt[:], st_m[:]), rd=(xk, "st_m"), wr=("lnsc",))
                    ctx.op("vector", lambda yb=ybf[t % 2]: nc.vector.scalar_tensor_tensor(
                        out=yb[:], in0=lnsc[:], scalar=1.0, in1=lnsc[:],
                        op0=ALU.mult, op1=ALU.mult, accum_out=st_v[:]),
                        rd=("lnsc",), wr=(yk, "st_v"))
                    ctx.op("vector", lambda: nc.vector.tensor_scalar(
                        out=st_v[:], in0=st_v[:], scalar1=1.0 / D, scalar2=EPS_LN,
                        op0=ALU.mult, op1=ALU.add), rd=("st_v",), wr=("st_v",))
                    ctx.op("vector", lambda: nc.vector.reciprocal(
                        st_v[:], st_v[:]), rd=("st_v",), wr=("st_v",))
                    ctx.op("scalar", lambda: nc.scalar.activation(
                        out=st_r[:], in_=st_v[:], func=AF.Sqrt), rd=("st_v",), wr=("st_r",))
                    ctx.op("vector", lambda yb=ybf[t % 2]: nc.vector.tensor_scalar_mul(
                        yb[:], lnsc[:], st_r[:]), rd=("lnsc", "st_r"), wr=(yk,))
                    for cch in range(DC):
                        ctx.op("tensor", lambda yb=ybf[t % 2], cch=cch: nc.tensor.transpose(
                            ps_tr[:], yb[:, cch * 128:(cch + 1) * 128], ident[:]),
                            rd=(yk, "ident"), wr=("ps_tr",))
                        ctx.op("vector", lambda cch=cch, t=t: nc.vector.tensor_copy(
                            yT[:, cch, t * 128:(t + 1) * 128], ps_tr[:]),
                            rd=("ps_tr",), wr=("yT",))
            ctx.barrier(omit=("q_gpsimd",))

            with contextlib.ExitStack() as p3:
                def a3(name, shape, dt):
                    return p3.enter_context(nc.sbuf_tensor(tag + name, shape, dt))
                wqs = [a3(f"wqs{i}", [128, DC, 128], BF16) for i in range(2)]
                wvs = [a3(f"wvs{i}", [128, DC, 512], BF16) for i in range(2)]
                ev = [a3(f"ev{i}", [128, 512], BF16) for i in range(2)]
                ps_p = [p3.enter_context(
                    nc.psum_tensor(tag + f"psp{i}", [128, 512], F32)) for i in range(2)]

                for ofc in range(16):
                    wk = f"wqs{ofc % 2}"
                    ctx.dma("sync", lambda ofc=ofc, b=wqs[ofc % 2]: nc.sync.dma_start(
                        out=b[:], in_=wq_r[:, :, ofc * 128:(ofc + 1) * 128]),
                        rd=("wqkv",), wr=(wk,), sem=f"w_g{ofc % 2}")
                    dst_h = qfm_hbm if ofc < 8 else kfm_hbm
                    cchunk = ofc % 8
                    for tb in range(4):
                        pk = f"ps_p{tb % 2}"
                        parts = [(lambda kd=kd, ofc=ofc, tb=tb: nc.tensor.matmul(
                            ps_p[tb % 2][:], wqs[ofc % 2][:, kd, :],
                            yT[:, kd, tb * 512:(tb + 1) * 512],
                            start=(kd == 0), stop=(kd == DC - 1)), (wk, "yT"))
                            for kd in range(DC)]
                        ctx.mm_group(parts, pk)
                        ek = f"ev{tb % 2}"
                        if ofc < 8:
                            ctx.op("vector", lambda tb=tb: nc.vector.tensor_scalar_mul(
                                ev[tb % 2][:], ps_p[tb % 2][:], 0.125), rd=(pk,), wr=(ek,))
                        else:
                            ctx.op("vector", lambda tb=tb: nc.vector.tensor_copy(
                                ev[tb % 2][:], ps_p[tb % 2][:]), rd=(pk,), wr=(ek,))
                        ctx.dma("sync", lambda tb=tb, cchunk=cchunk, dst_h=dst_h,
                            b=ev[tb % 2]: nc.sync.dma_start(
                                out=dst_h.ap()[:, cchunk, tb * 512:(tb + 1) * 512],
                                in_=b[:]), rd=(ek,), wr=(f"spl{ofc}_{tb}",),
                            sem=f"s_sp{tb % 2}")
                for vh in range(2):
                    wk = f"wvs{vh}"
                    ctx.dma("sync", lambda vh=vh, b=wvs[vh]: nc.sync.dma_start(
                        out=b[:], in_=wq_r[:, :, 2 * D + vh * 512:2 * D + (vh + 1) * 512]),
                        rd=("wqkv",), wr=(wk,), sem=f"w_u{vh}")
                    for t in range(NT):
                        pk = f"ps_p{t % 2}"
                        parts = [(lambda kd=kd, vh=vh, t=t: nc.tensor.matmul(
                            ps_p[t % 2][:], yT[:, kd, t * 128:(t + 1) * 128],
                            wvs[vh][:, kd, :],
                            start=(kd == 0), stop=(kd == DC - 1)), (wk, "yT"))
                            for kd in range(DC)]
                        ctx.mm_group(parts, pk)
                        ek = f"ev{t % 2}"
                        ctx.op("vector", lambda t=t: nc.vector.tensor_copy(
                            ev[t % 2][:], ps_p[t % 2][:]), rd=(pk,), wr=(ek,))
                        ctx.dma("sync", lambda t=t, vh=vh, b=ev[t % 2]: nc.sync.dma_start(
                            out=vtm_hbm.ap()[:, t, vh * 512:(vh + 1) * 512], in_=b[:]),
                            rd=(ek,), wr=(f"spv{vh}_{t}",), sem=f"s_sp{t % 2}")
            ctx.barrier(omit=("q_gpsimd",))

            with contextlib.ExitStack() as p4:
                def a4(name, shape, dt):
                    return p4.enter_context(nc.sbuf_tensor(tag + name, shape, dt))
                poolQ = a4("poolQ", [128, DC, PACK_W], BF16)
                poolK = a4("poolK", [128, DC, PACK_W], BF16)
                gpool = a4("gpool", [128, DC, 56], BF16)
                pch = [a4(f"pch{i}", [128, 1024], F32) for i in range(2)]
                agts = a4("agts", [16, D], F32)
                agim = a4("agim", [128, D], BF16)
                halo = a4("halo", [128, DC, 32], BF16)
                khsel = a4("khsel", [32, D], BF16)
                kvsrc = a4("kvsrc", [128, DC, 48], BF16)
                qsrc = a4("qsrc", [128, DC, 40], BF16)
                wcs = [a4(f"wcs{i}", [128, DC, 128], BF16) for i in range(2)]
                wvc = [a4("wvc0", [128, DC, 512], BF16)] * 2
                ps_t32 = p4.enter_context(nc.psum_tensor(tag + "pt32", [128, 128], F32))
                ps_h = p4.enter_context(nc.psum_tensor(tag + "psh", [128, 512], F32))
                ps_ht = p4.enter_context(nc.psum_tensor(tag + "psht", [128, 128], BF16))

                ctx.op("vector", lambda: nc.vector.memset(poolQ[:], 0.0), wr=("poolQ",))
                ctx.op("vector", lambda: nc.vector.memset(poolK[:], 0.0), wr=("poolK",))

                for cch in range(DC):
                    cur = 0
                    for l in range(1, 9):
                        n = T >> l
                        if l == 1:
                            ctx.op("vector", lambda cch=cch: nc.vector.tensor_add(
                                pch[0][:, 0:1024], yT[:, cch, 0:T:2], yT[:, cch, 1:T:2]),
                                rd=("yT",), wr=("pch0",))
                            cur = 0
                        else:
                            ctx.op("vector", lambda cur=cur, n=n: nc.vector.tensor_add(
                                pch[1 - cur][:, 0:n], pch[cur][:, 0:2 * n:2],
                                pch[cur][:, 1:2 * n:2]),
                                rd=(f"pch{cur}",), wr=(f"pch{1 - cur}",))
                            cur = 1 - cur
                        src = pch[cur]
                        sk = f"pch{cur}"
                        if l in LVLS:
                            nb = n // 16
                            off = LVL_PACK_OFF[l]
                            hw = (nb // 2) * 16
                            ctx.op("vector", lambda src=src, cch=cch, off=off, n=n, hw=hw:
                                nc.vector.tensor_copy(
                                    poolQ[:, cch, off:off + hw],
                                    src[:, 0:n].rearrange("p (b x) -> p b x", x=32)[:, :, 16:32]),
                                rd=(sk,), wr=("poolQ",))
                            ctx.op("vector", lambda src=src, cch=cch, off=off, n=n, hw=hw:
                                nc.vector.tensor_copy(
                                    poolK[:, cch, off:off + hw],
                                    src[:, 0:n].rearrange("p (b x) -> p b x", x=32)[:, :, 0:16]),
                                rd=(sk,), wr=("poolK",))
                        if l >= 6:
                            off = GP_OFF[l]
                            ctx.op("vector", lambda src=src, cch=cch, off=off, n=n:
                                nc.vector.tensor_copy(
                                    gpool[:, cch, off:off + n], src[:, 0:n]),
                                rd=(sk,), wr=("gpool",))
                        if l == 7:
                            ctx.op("tensor", lambda src=src: nc.tensor.transpose(
                                ps_t32[0:16, 0:128], src[:, 0:16], ident32[:]),
                                rd=(sk, "ident32"), wr=("ps_t32",))
                            ctx.op("vector", lambda cch=cch: nc.vector.tensor_copy(
                                agts[:, cch * 128:(cch + 1) * 128], ps_t32[0:16, 0:128]),
                                rd=("ps_t32",), wr=("agts",))

                ctx.dma("sync", lambda: nc.sync.dma_start(
                    out=agin_hbm.ap(), in_=agts[:]), rd=("agts",), wr=("agin",),
                    sem="s_ag")
                ctx.cc(lambda: nc.gpsimd.collective_compute(
                    "AllGather", ALU.bypass, replica_groups=[list(range(NCORES))],
                    ins=[agin_hbm.ap().opt()], outs=[agout_hbm.ap().opt()]),
                    rd=("agin",), wr=("agout",))
                ctx.dma("gpsimd", lambda: nc.gpsimd.dma_start(
                    out=agim[:], in_=agout_hbm.ap()), rd=("agout",), wr=("agim",),
                    sem="s_ag2")
                for half in range(2):
                    ctx.mm_group([(lambda half=half: nc.tensor.matmul(
                        ps_h[0:32, 0:512], selc[:],
                        agim[:, half * 512:(half + 1) * 512],
                        start=True, stop=True), ("selc", "agim"))], "ps_h")
                    ctx.op("vector", lambda half=half: nc.vector.tensor_copy(
                        khsel[:, half * 512:(half + 1) * 512], ps_h[0:32, 0:512]),
                        rd=("ps_h",), wr=("khsel",))
                for cch in range(DC):
                    ctx.op("tensor", lambda cch=cch: nc.tensor.transpose(
                        ps_ht[0:128, 0:32], khsel[:, cch * 128:(cch + 1) * 128],
                        ident[0:32, 0:32]), rd=("khsel", "ident"), wr=("ps_ht",))
                    ctx.op("vector", lambda cch=cch: nc.vector.tensor_copy(
                        halo[:, cch, :], ps_ht[0:128, 0:32]), rd=("ps_ht",), wr=("halo",))
                ctx.op("vector", lambda: nc.vector.tensor_copy(
                    kvsrc[:, :, 0:16], gpool[:, :, 0:16]), rd=("gpool",), wr=("kvsrc",))
                ctx.op("vector", lambda: nc.vector.tensor_copy(
                    kvsrc[:, :, 16:48], halo[:]), rd=("halo",), wr=("kvsrc",))
                ctx.op("vector", lambda: nc.vector.tensor_copy(
                    qsrc[:, :, 0:40], gpool[:, :, 16:56]), rd=("gpool",), wr=("qsrc",))

                ctx.op("vector", lambda: nc.vector.memset(
                    vc[:, :, 1024:1025], 1.0), wr=("vc",))


                for ofc in range(16):
                    wk = f"wcs{ofc % 2}"
                    ctx.dma("sync", lambda ofc=ofc, b=wcs[ofc % 2]: nc.sync.dma_start(
                        out=b[:], in_=wq_r[:, :, ofc * 128:(ofc + 1) * 128]),
                        rd=("wqkv",), wr=(wk,), sem=f"w_g{ofc % 2}")
                    isq = ofc < 8
                    cchunk = ofc % 8
                    dstc, dkey = (qc, "qc") if isq else (kc, "kc")
                    srcp, spk = (poolQ, "poolQ") if isq else (poolK, "poolK")
                    for nb0 in range(3):
                        n0 = nb0 * 512
                        n1 = min(n0 + 512, PACK_W)
                        parts = [(lambda kd=kd, ofc=ofc, n0=n0, n1=n1, srcp=srcp:
                            nc.tensor.matmul(
                                ps_h[:, 0:n1 - n0], wcs[ofc % 2][:, kd, :],
                                srcp[:, kd, n0:n1],
                                start=(kd == 0), stop=(kd == DC - 1)),
                            (wk, spk)) for kd in range(DC)]
                        ctx.mm_group(parts, "ps_h")
                        ctx.op("vector", lambda cchunk=cchunk, n0=n0, n1=n1, dstc=dstc:
                            nc.vector.tensor_copy(
                                dstc[:, cchunk, n0:n1], ps_h[:, 0:n1 - n0]),
                            rd=("ps_h",), wr=(dkey,))
                    srcg, sgk = (qsrc, "qsrc") if isq else (kvsrc, "kvsrc")
                    ng = 40 if isq else 48
                    dstg, dgk = (qg, "qg") if isq else (kg, "kg")
                    parts = [(lambda kd=kd, ofc=ofc, srcg=srcg, ng=ng:
                        nc.tensor.matmul(
                            ps_h[:, 0:ng], wcs[ofc % 2][:, kd, :],
                            srcg[:, kd, 0:ng],
                            start=(kd == 0), stop=(kd == DC - 1)),
                        (wk, sgk)) for kd in range(DC)]
                    ctx.mm_group(parts, "ps_h")
                    ctx.op("vector", lambda cchunk=cchunk, ng=ng, dstg=dstg, dgk=dgk:
                        nc.vector.tensor_copy(dstg[:, cchunk, 0:ng], ps_h[:, 0:ng]),
                        rd=("ps_h",), wr=(dgk,))
                for vh in range(2):
                    wk = "wvc0"
                    ctx.dma("sync", lambda vh=vh, b=wvc[vh]: nc.sync.dma_start(
                        out=b[:], in_=wq_r[:, :, 2 * D + vh * 512:2 * D + (vh + 1) * 512]),
                        rd=("wqkv",), wr=(wk,), sem=f"w_u{vh}")
                    for cb in range(9):
                        parts = [(lambda kd=kd, vh=vh, cb=cb: nc.tensor.matmul(
                            ps_h[:], poolK[:, kd, cb * 128:(cb + 1) * 128],
                            wvc[vh][:, kd, :],
                            start=(kd == 0), stop=(kd == DC - 1)),
                            (wk, "poolK")) for kd in range(DC)]
                        ctx.mm_group(parts, "ps_h")
                        ctx.op("vector", lambda cb=cb, vh=vh: nc.vector.tensor_copy(
                            vc[:, cb, vh * 512:(vh + 1) * 512], ps_h[:]),
                            rd=("ps_h",), wr=("vc",))
                    for lv in range(3):
                        parts = [(lambda kd=kd, vh=vh, lv=lv: nc.tensor.matmul(
                            ps_h[0:16, :], kvsrc[:, kd, lv * 16:(lv + 1) * 16],
                            wvc[vh][:, kd, :],
                            start=(kd == 0), stop=(kd == DC - 1)),
                            (wk, "kvsrc")) for kd in range(DC)]
                        ctx.mm_group(parts, "ps_h")
                        ctx.op("vector", lambda vh=vh, lv=lv: nc.vector.tensor_copy(
                            vg[:, lv, vh * 512:(vh + 1) * 512], ps_h[0:16, :]),
                            rd=("ps_h",), wr=("vg",))
            ctx.barrier(omit=("q_gpsimd",))

        with contextlib.ExitStack() as pb:
            def bl(name, shape, dt):
                return pb.enter_context(nc.sbuf_tensor(tag + name, shape, dt))
            scc_sb = bl("scc", [128, 9, T], BF16)
            sc678_sb = bl("s678", [16, NT, 3, 128], BF16)
            ctx.dma("sync", lambda: nc.sync.dma_start(
                out=scc_sb[:], in_=scc_in.rearrange("(g p) t -> p g t", p=128)),
                wr=("scc",), sem="s_cat")
            ctx.dma("sync", lambda: nc.sync.dma_start(
                out=sc678_sb[:], in_=sc678_in.rearrange("t l q f -> q t l f")),
                wr=("sc678",), sem="s_cat")
            ctx.settle(("scc", "sc678"), ("s_cat",))
            q_h = [bl(f"qh{i}", [64, T], BF16) for i in range(2)]
            k_h = [bl(f"kh{i}", [64, T], BF16) for i in range(2)]
            v_h = [bl(f"vh{i}", [128, NT, 65], BF16) for i in range(2)]
            Ycomp = bl("Ycomp", [128, 9, 65], BF16)
            Y678 = bl("Y678", [16, 3, 65], BF16)
            ctx.op("vector", lambda: nc.vector.memset(Y678[:], 0.0),
                   rd=(), wr=("Y678",))
            At_sb = bl("At", [128, 128], BF16)
            sm_c = bl("smc", [128, 128], F32)
            Ab = bl("Ab", [128, 128], BF16)
            mx = bl("mx", [128, 1], F32)
            bias = bl("bias", [128, 1], F32)
            vy = bl("vy", [128, 65], BF16)
            nrm = bl("nrm", [128, 64], BF16)
            avs = bl("avs", [128, 1], F32)
            ps_sc = pb.enter_context(nc.psum_tensor(tag + "pssc", [128, 128], F32))
            ps_at = pb.enter_context(nc.psum_tensor(tag + "psat", [128, 128], BF16))
            ps_y = pb.enter_context(nc.psum_tensor(tag + "psy", [128, 65], F32))
            ps_nt = pb.enter_context(nc.psum_tensor(tag + "psnt", [128, 128], BF16))

            for i in range(2):
                ctx.op("vector", lambda i=i: nc.vector.memset(
                    v_h[i][:, :, 64:65], 1.0), wr=(f"vhones{i}",))
            ctx.op("vector", lambda: nc.vector.memset(vy[:, 64:65], 1.0), wr=("vy",))

            for h in range(HEADS):
                hrow = (h % 2) * 64
                hch = h // 2
                qk, kk, vk = f"q_h{h % 2}", f"k_h{h % 2}", f"v_h{h % 2}"
                ctx.dma("sync", lambda h=h, b=q_h[h % 2], hrow=hrow, hch=hch:
                    nc.sync.dma_start(out=b[:],
                        in_=qfm_hbm.ap()[hrow:hrow + 64, hch, :]),
                    rd=tuple(f"spl{hch}_{tb}" for tb in range(4)), wr=(qk,),
                    sem=f"s_qh{h % 2}")
                ctx.dma("sync", lambda h=h, b=k_h[h % 2], hrow=hrow, hch=hch:
                    nc.sync.dma_start(out=b[:],
                        in_=kfm_hbm.ap()[hrow:hrow + 64, hch, :]),
                    rd=tuple(f"spl{8 + hch}_{tb}" for tb in range(4)), wr=(kk,),
                    sem=f"s_kh{h % 2}")
                ctx.dma("sync", lambda h=h, b=v_h[h % 2]: nc.sync.dma_start(
                    out=b[:, :, 0:64], in_=vtm_hbm.ap()[:, :, h * 64:(h + 1) * 64]),
                    rd=tuple(f"spv{vh}_{t}" for vh in range(2) for t in range(0, NT, 5))
                    + (f"vhones{h % 2}",),
                    wr=(vk,), sem=f"s_vh{h % 2}")

                for g in range(9):
                    cl = CL[GROUP_LVL[g]]
                    ctx.mm_group([(lambda g=g, hrow=hrow, hch=hch: nc.tensor.matmul(
                        ps_sc[:], qc[hrow:hrow + 64, hch, g * 128:(g + 1) * 128],
                        kc[hrow:hrow + 64, hch, g * 128:(g + 1) * 128],
                        start=True, stop=True), ("qc", "kc"))], "ps_sc")
                    ctx.op("vector", lambda: nc.vector.tensor_add(
                        sm_c[:], ps_sc[:], mcc[:]), rd=("ps_sc", "mcc"), wr=("sm_c",))
                    ctx.op("vector", lambda: nc.vector.reduce_max(
                        mx[:], sm_c[:], axis=mybir.AxisListType.X),
                        rd=("sm_c",), wr=("mx",))
                    ctx.op("vector", lambda cl=cl: nc.vector.tensor_scalar_mul(
                        bias[:], mx[:], -cl), rd=("mx",), wr=("bias",))
                    ctx.op("scalar", lambda cl=cl: nc.scalar.activation(
                        out=Ab[:], in_=sm_c[:], func=AF.Exp, bias=bias[:], scale=cl),
                        rd=("sm_c", "bias"), wr=("Ab",))
                    ctx.op("tensor", lambda: nc.tensor.transpose(
                        ps_at[:], Ab[:], ident[:]), rd=("Ab", "ident"), wr=("ps_at",))
                    ctx.op("vector", lambda: nc.vector.tensor_copy(
                        At_sb[:], ps_at[:]), rd=("ps_at",), wr=("At_sb",))
                    ctx.op("vector", lambda g=g, h=h: nc.vector.tensor_copy(
                        vy[:, 0:64], vc[:, g, h * 64:(h + 1) * 64]),
                        rd=("vc",), wr=("vy",))
                    ctx.mm_group([(lambda: nc.tensor.matmul(
                        ps_y[:], At_sb[:], vy[:], start=True, stop=True),
                        ("At_sb", "vy"))], "ps_y")
                    ctx.op("vector", lambda g=g: nc.vector.tensor_copy(
                        Ycomp[:, g, :], ps_y[:]), rd=("ps_y",), wr=("Ycomp",))

                for lvi, (lv, q0, nq, nk) in enumerate(
                        ((6, 0, 16, 16), (7, 16, 16, 16), (8, 32, 8, 16))):
                    cl = CL[lv]
                    k0 = lvi * 16
                    ctx.mm_group([(lambda q0=q0, nq=nq, k0=k0, nk=nk, hrow=hrow,
                        hch=hch: nc.tensor.matmul(
                            ps_sc[0:nq, 0:nk], qg[hrow:hrow + 64, hch, q0:q0 + nq],
                            kg[hrow:hrow + 64, hch, k0:k0 + nk],
                            start=True, stop=True), ("qg", "kg"))], "ps_sc")
                    ctx.op("vector", lambda nq=nq, nk=nk: nc.vector.tensor_copy(
                        sm_c[0:nq, 0:nk], ps_sc[0:nq, 0:nk]), rd=("ps_sc",), wr=("sm_c",))
                    ctx.op("vector", lambda nq=nq, nk=nk: nc.vector.reduce_max(
                        mx[0:nq, :], sm_c[0:nq, 0:nk], axis=mybir.AxisListType.X),
                        rd=("sm_c",), wr=("mx",))
                    ctx.op("vector", lambda cl=cl, nq=nq: nc.vector.tensor_scalar_mul(
                        bias[0:nq, :], mx[0:nq, :], -cl), rd=("mx",), wr=("bias",))
                    ctx.op("scalar", lambda cl=cl, nq=nq, nk=nk: nc.scalar.activation(
                        out=Ab[0:nq, 0:nk], in_=sm_c[0:nq, 0:nk], func=AF.Exp,
                        bias=bias[0:nq, :], scale=cl), rd=("sm_c", "bias"), wr=("Ab",))
                    ctx.op("tensor", lambda nq=nq, nk=nk: nc.tensor.transpose(
                        ps_at[0:nk, 0:nq], Ab[0:nq, 0:nk], ident[0:nq, 0:nq]),
                        rd=("Ab", "ident"), wr=("ps_at",))
                    ctx.op("vector", lambda nq=nq, nk=nk: nc.vector.tensor_copy(
                        At_sb[0:nk, 0:nq], ps_at[0:nk, 0:nq]), rd=("ps_at",), wr=("At_sb",))
                    parts = [(lambda nq=nq, nk=nk, lvi=lvi, h=h: nc.tensor.matmul(
                        ps_y[0:nq, 0:64], At_sb[0:nk, 0:nq],
                        vg[:, lvi, h * 64:(h + 1) * 64],
                        start=True, stop=False), ("At_sb", "vg")),
                        (lambda nq=nq, nk=nk: nc.tensor.matmul(
                            ps_y[0:nq, 64:65], At_sb[0:nk, 0:nq], onescol[0:16, :],
                            start=False, stop=True), ("At_sb", "onescol"))]
                    ctx.mm_group(parts, "ps_y")
                    ctx.op("vector", lambda lvi=lvi, nq=nq: nc.vector.tensor_copy(
                        Y678[0:nq, lvi, :], ps_y[0:nq, 0:65]), rd=("ps_y",), wr=("Y678",))

                for t in range(NT):
                    ctx.mm_group([(lambda t=t, h=h: nc.tensor.matmul(
                        ps_sc[:], q_h[h % 2][:, t * 128:(t + 1) * 128],
                        k_h[h % 2][:, t * 128:(t + 1) * 128],
                        start=True, stop=True), (qk, kk))], "ps_sc")
                    ctx.op("vector", lambda: nc.vector.tensor_add(
                        sm_c[:], ps_sc[:], m0c[:]), rd=("ps_sc", "m0c"), wr=("sm_c",))
                    ctx.op("vector", lambda: nc.vector.reduce_max(
                        mx[:], sm_c[:], axis=mybir.AxisListType.X),
                        rd=("sm_c",), wr=("mx",))
                    ctx.op("vector", lambda: nc.vector.tensor_scalar_mul(
                        bias[:], mx[:], -1.0), rd=("mx",), wr=("bias",))
                    ctx.op("scalar", lambda: nc.scalar.activation(
                        out=Ab[:], in_=sm_c[:], func=AF.Exp, bias=bias[:], scale=1.0),
                        rd=("sm_c", "bias"), wr=("Ab",))
                    ctx.op("tensor", lambda: nc.tensor.transpose(
                        ps_at[:], Ab[:], ident[:]), rd=("Ab", "ident"), wr=("ps_at",))
                    ctx.op("vector", lambda: nc.vector.tensor_copy(
                        At_sb[:], ps_at[:]), rd=("ps_at",), wr=("At_sb",))
                    parts = [(lambda t=t, h=h: nc.tensor.matmul(
                        ps_y[:], At_sb[:], v_h[h % 2][:, t, :],
                        start=True, stop=False), ("At_sb", vk))]
                    for g in OV[t]:
                        parts.append((lambda g=g, t=t: nc.tensor.matmul(
                            ps_y[:], scc_sb[:, g, t * 128:(t + 1) * 128],
                            Ycomp[:, g, :], start=False, stop=False),
                            ("scc", "Ycomp")))
                    for lvi in range(3):
                        parts.append((lambda t=t, lvi=lvi: nc.tensor.matmul(
                            ps_y[:], sc678_sb[:, t, lvi, :], Y678[:, lvi, :],
                            start=False, stop=(lvi == 2)), ("sc678", "Y678")))
                    ctx.mm_group(parts, "ps_y")
                    ctx.op("vector", lambda: nc.vector.tensor_scalar(
                        out=avs[:], in0=ps_y[:, 64:65], scalar1=EPS_ATT, scalar2=None,
                        op0=ALU.add), rd=("ps_y",), wr=("avs",))
                    ctx.op("vector", lambda: nc.vector.reciprocal(
                        avs[:], avs[:]), rd=("avs",), wr=("avs",))
                    ctx.op("vector", lambda: nc.vector.tensor_scalar_mul(
                        nrm[:], ps_y[:, 0:64], avs[:]), rd=("ps_y", "avs"), wr=("nrm",))
                    ctx.op("tensor", lambda: nc.tensor.transpose(
                        ps_nt[0:64, 0:128], nrm[:], ident[:]),
                        rd=("nrm", "ident"), wr=("ps_nt",))
                    ctx.op("vector", lambda t=t, hrow=hrow, hch=hch: nc.vector.tensor_copy(
                        normT[hrow:hrow + 64, hch, t * 128:(t + 1) * 128],
                        ps_nt[0:64, 0:128]), rd=("ps_nt",), wr=("normT",))
            ctx.barrier(omit=("q_gpsimd",))

        with contextlib.ExitStack() as pc:
            def cl_(name, shape, dt):
                return pc.enter_context(nc.sbuf_tensor(tag + name, shape, dt))
            wos = cl_("wos", [128, DC, D], BF16)
            xt2 = [cl_(f"xo{i}", [128, D], F32) for i in range(2)]
            x3t = [cl_(f"x3{i}", [128, D], F32) for i in range(2)]
            ps_o = [pc.enter_context(
                nc.psum_tensor(tag + f"pso{i}", [128, 512], F32)) for i in range(2)]
            ctx.dma("sync", lambda: nc.sync.dma_start(
                out=wos[:], in_=wo_r[:]), rd=("wout",), wr=("wos",), sem="s_wd")
            x3_r = (out if x_dst_is_out else x3_hbm.ap()).rearrange(
                "(t p) d -> p t d", p=128)
            for t in range(NT):
                xk = f"xo{t % 2}"
                ctx.dma("sync", lambda t=t, b=xt2[t % 2]: nc.sync.dma_start(
                    out=b[:], in_=x2_r[:, t, :]), rd=("x_dst",), wr=(xk,),
                    sem=f"s_xt{t % 2}")
                for half in range(2):
                    pok = f"ps_o{half}"
                    parts = [(lambda kd=kd, t=t, half=half: nc.tensor.matmul(
                        ps_o[half][:], normT[:, kd, t * 128:(t + 1) * 128],
                        wos[:, kd, half * 512:(half + 1) * 512],
                        start=(kd == 0), stop=False), ("normT", "wos"))
                        for kd in range(DC)]
                    parts.append((lambda half=half: nc.tensor.matmul(
                        ps_o[half][:], onesb[:],
                        bout_sb[:, half * 512:(half + 1) * 512],
                        start=False, stop=True), ("onesb", "bout")))
                    ctx.mm_group(parts, pok)
                    ctx.op("vector", lambda t=t, half=half: nc.vector.tensor_add(
                        x3t[t % 2][:, half * 512:(half + 1) * 512], ps_o[half][:],
                        xt2[t % 2][:, half * 512:(half + 1) * 512]),
                        rd=(pok, xk), wr=(f"x3{t % 2}",))
                ctx.dma("sync", lambda t=t, b=x3t[t % 2]: nc.sync.dma_start(
                    out=x3_r[:, t, :], in_=b[:]), rd=(f"x3{t % 2}",),
                    wr=("x_dst3",), sem=f"s_x3{t % 2}")
        ctx.barrier()



def kernel(L0_attn_x, L0_attn_w_qkv, L0_attn_w_out, L0_attn_b_out,
           L0_ffn_w_queries, L0_ffn_keys, L0_ffn_down_embed, L0_ffn_up_embed,
           L0_ffn_w_gate, L0_ffn_w_up, L0_ffn_w_down, _stages=3):
    x = np.ascontiguousarray(np.asarray(L0_attn_x))
    assert x.shape == (2, 8192, D)

    wqkvT = np.ascontiguousarray(np.asarray(L0_attn_w_qkv).T)
    woutT = np.ascontiguousarray(np.asarray(L0_attn_w_out).T)
    bout = np.ascontiguousarray(np.asarray(L0_attn_b_out).reshape(1, D))
    wqT = np.ascontiguousarray(np.asarray(L0_ffn_w_queries).T)
    keys = np.asarray(L0_ffn_keys)
    keysT = np.ascontiguousarray(keys.transpose(0, 3, 2, 1).reshape(128, 2, 128))
    tbl = np.ascontiguousarray(np.concatenate(
        [np.asarray(L0_ffn_down_embed), np.asarray(L0_ffn_up_embed)], axis=1))
    wgT = np.ascontiguousarray(np.asarray(L0_ffn_w_gate).T)
    wuT = np.ascontiguousarray(np.asarray(L0_ffn_w_up).T)
    wdT = np.ascontiguousarray(np.asarray(L0_ffn_w_down).T)

    in_maps = []
    metas = []
    for cid in range(NCORES):
        b, c = divmod(cid, 4)
        cc = _host_consts(cid)
        metas.append(cc)
        in_maps.append({
            "x_in": np.ascontiguousarray(x[b, c * T:(c + 1) * T]),
            "wqkvT": wqkvT, "woutT": woutT, "bout": bout, "wqT": wqT,
            "keysT": keysT, "tbl_in": tbl, "wgT": wgT, "wuT": wuT, "wdT": wdT,
            "ident": cc["ident"], "ident32": np.eye(128, dtype=np.float32),
            "m0c": cc["M0"], "mcc": cc["MC"],
            "iota64": cc["iota64"], "ones_bf": cc["ones_bf"],
            "scc": cc["ScC"], "sc678": cc["Sc678"], "sel": cc["Sel"],
            "ones_col": np.ones((128, 1), dtype=ml_dtypes.bfloat16),
        })

    nc = build_nc(metas[0], stages=_stages)
    res = run_bass_kernel_spmd(nc, in_maps, core_ids=list(range(NCORES)))
    outp = np.empty((2, 8192, D), np.float32)
    for cid in range(NCORES):
        b, c = divmod(cid, 4)
        outp[b, c * T:(c + 1) * T] = res.results[cid]["out"]
    if DEBUG:
        dbg = {k: [res.results[cid].get("dbg_" + k) for cid in range(NCORES)]
               for k in DEBUG}
        return outp, dbg
    return outp
